# version 04
# baseline (speedup 1.0000x reference)
import math
import numpy as np
import ml_dtypes
import concourse.bass as bass
import concourse.mybir as mybir
from concourse.bass import broadcast_tensor_aps
from concourse.bass_utils import run_bass_kernel_spmd


F32 = mybir.dt.float32
BF16 = mybir.dt.bfloat16
ALU = mybir.AluOpType
AF = mybir.ActivationFunctionType
AX = mybir.AxisListType

ENGS = ("T", "V", "S", "P", "Q")


class KB:
    NPOOL = 10

    def __init__(self, nc: bass.Bass):
        self.nc = nc
        self.instrs = []
        self.regions = {}
        self.eng_seq = {}
        self.stream_count = {e: 0 for e in ENGS}
        for e in ENGS:
            self.stream_count[e + "!"] = 0
        self.stream_of = {}
        self.snapshot = {}
        self.clock = {e: {} for e in ENGS}
        self.finalized = False

    def _record(self, eng, fn, reads, writes, dma=False, pe_accum=False, tag="",
                inc_n=None, stream=None, meta=None):
        assert not self.finalized
        iid = len(self.instrs)
        if stream is None:
            stream = (eng + "!") if dma else eng
        if stream not in self.stream_count:
            self.stream_count[stream] = 0
        base_stream = stream
        if dma:
            seq_tmp = self.stream_count[base_stream]
            stream = (base_stream, seq_tmp % self.NPOOL)
        deps = set()
        for k in reads:
            reg = self.regions.get(k)
            if reg and reg["w"] is not None:
                deps.add(reg["w"])
        for k in writes:
            reg = self.regions.setdefault(k, {"w": None, "r": []})
            if reg["w"] is not None:
                deps.add(reg["w"])
            deps.update(reg["r"])
        if eng == "T":
            deps = {d for d in deps
                    if not (self.instrs[d]["eng"] == "T" and not self.instrs[d]["dma"])}
        clk = dict(self.clock[eng])
        waits = {}
        for d in sorted(deps):
            ds = self.stream_of[d]
            dseq = self.eng_seq[d]
            if clk.get(ds, -1) >= dseq:
                continue
            waits[ds] = max(waits.get(ds, -1), dseq)
            for s2, v2 in self.snapshot[d].items():
                if clk.get(s2, -1) < v2:
                    clk[s2] = v2
            clk[ds] = max(clk.get(ds, -1), dseq)
        self.clock[eng] = clk
        seq = self.stream_count[base_stream]
        self.stream_count[base_stream] = seq + 1
        self.instrs.append({"eng": eng, "fn": fn, "dma": dma, "waits": waits,
                            "tag": tag, "inc_n": inc_n, "stream": stream,
                            "base": base_stream, "meta": meta})
        self.eng_seq[iid] = seq
        self.stream_of[iid] = stream
        self.snapshot[iid] = dict(clk)
        for k in reads:
            reg = self.regions.setdefault(k, {"w": None, "r": []})
            reg["r"].append(iid)
        for k in writes:
            reg = self.regions[k]
            reg["w"] = iid
            reg["r"] = []
        return iid

    def op(self, eng, fn, reads=(), writes=(), tag=""):
        return self._record(eng, fn, list(reads), list(writes), tag=tag)

    def mm(self, fn, reads=(), writes=(), tag="", meta=None):
        return self._record("T", fn, list(reads), list(writes), tag=tag, meta=meta)

    def check_psum_groups(self):
        state = {}
        errs = []
        for iid, ins in enumerate(self.instrs):
            m = ins.get("meta")
            if m is None:
                continue
            bank, st, sp = m
            if st:
                if state.get(bank):
                    errs.append((iid, ins["tag"], bank, "start-while-open"))
                state[bank] = True
            else:
                if not state.get(bank):
                    errs.append((iid, ins["tag"], bank, "accum-not-started"))
            if sp:
                state[bank] = False
        return errs

    def dma(self, eng, out_ap, in_ap, reads=(), writes=(), tag=""):
        fn = lambda e, o=out_ap, i=in_ap: e.dma_start(out=o, in_=i)
        return self._record(eng, fn, list(reads), list(writes), dma=True, tag=tag)

    def collective(self, fn, reads=(), writes=(), tag="cc"):
        return self._record("P", fn, list(reads), list(writes), dma=True,
                            stream="C!", tag=tag)

    def barrier_region(self, key, eng, reads):
        return self.op(eng, lambda e: e.engine_nop(), reads=reads, writes=[key],
                       tag="barrier")

    def final_wait(self, keys):
        return self.op("V", lambda e: e.engine_nop(), reads=keys, tag="final")

    def phase_flush(self):
        deps = set()
        for reg in self.regions.values():
            if reg["w"] is not None:
                deps.add(reg["w"])
            deps.update(reg["r"])
        self.regions = {}
        ids = sorted(deps)
        for eng in ENGS:
            clk = dict(self.clock[eng])
            waits = {}
            for d in ids:
                ds = self.stream_of[d]
                dseq = self.eng_seq[d]
                if clk.get(ds, -1) >= dseq:
                    continue
                waits[ds] = max(waits.get(ds, -1), dseq)
                for s2, v2 in self.snapshot[d].items():
                    if clk.get(s2, -1) < v2:
                        clk[s2] = v2
                clk[ds] = max(clk.get(ds, -1), dseq)
            self.clock[eng] = clk
            iid = len(self.instrs)
            stream = eng
            seq = self.stream_count[stream]
            self.stream_count[stream] = seq + 1
            self.instrs.append({"eng": eng, "fn": None, "dma": False, "waits": waits,
                                "tag": "flush", "inc_n": None, "stream": stream})
            self.eng_seq[iid] = seq
            self.stream_of[iid] = stream
            self.snapshot[iid] = dict(clk)

    def finalize(self):
        assert not self.finalized
        self.finalized = True
        nc = self.nc
        needed = {}
        for ins in self.instrs:
            for s, v in ins["waits"].items():
                needed.setdefault(s, set()).add(v)
        comp_streams = sorted(s for s in needed if isinstance(s, str))
        lane_streams = sorted({self.stream_of[i] for i, ins in enumerate(self.instrs)
                               if ins["dma"]}, key=str)
        sems = {}
        for s in comp_streams:
            sems[s] = nc.alloc_semaphore(name="kb_" + s)
        for s in lane_streams:
            sems[s] = nc.alloc_semaphore(name="kb_%s%d" % (s[0].replace("!", "d"), s[1]))
        self.sems = sems
        inc_rank = {s: {q: i + 1 for i, q in enumerate(sorted(needed[s]))}
                    for s in comp_streams}
        eng_lists = {e: [] for e in ENGS}
        for iid, ins in enumerate(self.instrs):
            eng_lists[ins["eng"]].append((iid, ins))
        NP = self.NPOOL

        def emit_for(eng_obj, eng_name):
            for iid, ins in eng_lists[eng_name]:
                for s, v in sorted(ins["waits"].items(), key=str):
                    if isinstance(s, tuple):
                        eng_obj.wait_ge(sems[s], 16 * (v // NP + 1))
                    else:
                        eng_obj.wait_ge(sems[s], inc_rank[s][v])
                stream = self.stream_of[iid]
                seq = self.eng_seq[iid]
                if ins["dma"] and seq >= NP:
                    eng_obj.wait_ge(sems[stream], 16 * ((seq - NP) // NP + 1))
                if ins["fn"] is None:
                    continue
                try:
                    with nc.named_scope(f"kbi{iid}.{ins['tag']}.{ins.get('meta')}"):
                        bi = ins["fn"](eng_obj)
                except Exception as ex:
                    raise RuntimeError(f"emit fail iid={iid} eng={eng_name} tag={ins['tag']} meta={ins.get('meta')}: {ex}") from ex
                if ins["dma"]:
                    bi.then_inc(sems[stream], 16)
                else:
                    rk = inc_rank.get(stream, {}).get(seq)
                    if rk is not None:
                        bi.then_inc(sems[stream], 1)

        with nc.Block() as block:
            if eng_lists["Q"]:
                @block.sync
                def _(e):
                    emit_for(e, "Q")
            if eng_lists["T"]:
                @block.tensor
                def _(e):
                    emit_for(e, "T")
            if eng_lists["V"]:
                @block.vector
                def _(e):
                    emit_for(e, "V")
            if eng_lists["S"]:
                @block.scalar
                def _(e):
                    emit_for(e, "S")
            if eng_lists["P"]:
                @block.gpsimd
                def _(e):
                    emit_for(e, "P")
        return self


D = 1024
NH = 16
HD = 64
DI = 2816
DI1 = 4096
EPS = 1e-5
PI = math.pi


def cdiv(a, b):
    return -(-a // b)


class Ctx:
    def __init__(self, nc, kb, bf_words=0, f32_words=0, int_words=0):
        self.nc = nc
        self.kb = kb
        self.cursor = {}
        self.arena = {}
        if bf_words:
            self.arena["bf"] = nc.alloc_sbuf_tensor("Abf", [128, bf_words], BF16)
            self.cursor["bf"] = 0
        if f32_words:
            self.arena["f32"] = nc.alloc_sbuf_tensor("Af32", [128, f32_words], F32)
            self.cursor["f32"] = 0
        if int_words:
            self.arena["i32"] = nc.alloc_sbuf_tensor("Ai32", [128, int_words], mybir.dt.int32)
            self.cursor["i32"] = 0
        self.marks = {}
        self.consts = {}
        self.peak = {}
        self.l1 = None

    def reset_if(self, name, kind):
        if (name, kind) in self.marks:
            self.cursor[kind] = self.marks[(name, kind)]

    def setup_psum(self):
        self.ps_f = [self.nc.alloc_psum_tensor("psf%d" % i, [128, 512], F32) for i in range(5)]
        self.ps_S = self.nc.alloc_psum_tensor("psS", [128, 1024], F32)
        self.ps_bf = self.nc.alloc_psum_tensor("psbf", [128, 1024], BF16)

    def cst(self, val):
        key = ("cst", float(val))
        if key not in self.consts:
            ap = self.carve("f32", 1)
            self.kb.op("V", lambda e, a=ap, v=float(val): e.memset(a, v), writes=[key])
            self.nc.const_aps.register(F32, float(val), ap)
            self.consts[key] = ap
        return key

    def carve(self, kind, width):
        off = self.cursor[kind]
        off = cdiv(off, 16) * 16
        assert off + width <= self.arena[kind].shape[1], \
            f"arena {kind} overflow: {off}+{width} > {self.arena[kind].shape[1]}"
        self.cursor[kind] = off + width
        self.peak[kind] = max(self.peak.get(kind, 0), self.cursor[kind])
        return self.arena[kind].ap()[:, off:off + width]

    def mark(self, name, kind):
        self.marks[(name, kind)] = self.cursor[kind]

    def reset(self, name, kind):
        self.cursor[kind] = self.marks[(name, kind)]



def t_copy(kb, eng, dst, src, rk, wk, tag=""):
    if eng == "S":
        kb.op("S", lambda e, o=dst, i=src: e.copy(o, i), reads=rk, writes=wk, tag=tag)
    else:
        kb.op(eng, lambda e, o=dst, i=src: e.tensor_copy(o, i), reads=rk, writes=wk, tag=tag)


def transpose128(ctx, src_ap, dst_ap, skeys, dkey, evac_eng="V"):
    kb = ctx.kb
    ps = ctx.ps_bf.ap()[:, 0:128]
    kb.mm(lambda e, o=ps, i=src_ap, idn=ctx.ident_bf: e.transpose(o, i, idn),
          reads=list(skeys) + ["c_ident"], writes=["psbf"], meta=("psbf", True, True))
    t_copy(kb, evac_eng, dst_ap, ps, ["psbf"], [dkey])


def bcast_row_f32(ctx, src_row_ap, dst_ap, src_keys, dkey, psn=0):
    kb = ctx.kb
    Wd = src_row_ap.shape[-1]
    ps = ctx.ps_f[psn]
    pk = "psf%d" % psn
    for c0 in range(0, Wd, 512):
        cw = min(512, Wd - c0)
        kb.mm(lambda e, o=ps.ap()[:, 0:cw], l=ctx.ones_f, r=src_row_ap[:, c0:c0 + cw]:
              e.matmul(o, l, r, start=True, stop=True),
              reads=list(src_keys) + ["c_ones"], writes=[pk], meta=(pk, True, True))
        t_copy(kb, "V", dst_ap[:, c0:c0 + cw], ps.ap()[:, 0:cw], [pk], [dkey])



def emit_ln_stats_tile(ctx, x_ap, xkey, msum_ap, ssum_ap, sq_ap, mkey, skey, sqkey):
    kb = ctx.kb
    kb.op("V", lambda e, o=msum_ap, i=x_ap: e.reduce_sum(o, i, axis=AX.X),
          reads=[xkey], writes=[mkey])
    kb.op("S", lambda e, o=sq_ap, i=x_ap: e.activation(o, i, AF.Square),
          reads=[xkey], writes=[sqkey])
    kb.op("V", lambda e, o=ssum_ap, i=sq_ap: e.reduce_sum(o, i, axis=AX.X),
          reads=[sqkey], writes=[skey])


def emit_ln_finish(ctx, msum_ap, ssum_ap, mean_ap, rstd_ap, tmp_ap, keys, n=D):
    kb = ctx.kb
    msk, ssk, meank, rstdk, tmpk = keys
    msk = list(msk) if isinstance(msk, (list, tuple)) and msk and isinstance(msk[0], tuple) else [msk]
    ssk = list(ssk) if isinstance(ssk, (list, tuple)) and ssk and isinstance(ssk[0], tuple) else [ssk]
    kb.op("V", lambda e, o=mean_ap, i=msum_ap: e.tensor_scalar(o, i, 1.0 / n, None, ALU.mult),
          reads=msk, writes=[meank])
    kb.op("V", lambda e, o=tmp_ap, i=mean_ap: e.tensor_tensor(o, i, i, ALU.mult),
          reads=[meank], writes=[tmpk])
    kb.op("V", lambda e, o=rstd_ap, i=ssum_ap, t=tmp_ap:
          e.scalar_tensor_tensor(o, i, 1.0 / n, t, ALU.mult, ALU.subtract),
          reads=ssk + [tmpk], writes=[("var", rstdk)])
    ck = ctx.cst(float(EPS))
    kb.op("S", lambda e, o=rstd_ap, i=rstd_ap: e.activation(o, i, AF.Ln, bias=float(EPS)),
          reads=[("var", rstdk), ck], writes=[("lnv", rstdk)])
    kb.op("S", lambda e, o=rstd_ap, i=rstd_ap: e.activation(o, i, AF.Exp, scale=-0.5),
          reads=[("lnv", rstdk)], writes=[rstdk])



def emit_quant_fp4(ctx, x_ap, xkey, out_bf_ap, outkey, W, tmps, tag=""):
    kb = ctx.kb
    nb = W // 16
    bm, rs2, sch, A, H, q, m = (tmps[k] for k in ("bm", "rs2", "sch", "A", "H", "q", "m"))
    k = lambda n: (tag, n)
    kb.op("V", lambda e, o=bm, i=x_ap.rearrange("p (b i) -> p b i", i=16):
          e.reduce_max(o, i, axis=AX.X, apply_absolute_value=True),
          reads=[xkey], writes=[k("bm")])
    kb.op("V", lambda e, o=sch, i=bm: e.tensor_scalar(o, i, 1.2e-11, 1.0 / 12, ALU.max, ALU.mult),
          reads=[k("bm")], writes=[k("sch")])
    kb.op("V", lambda e, o=rs2, i=sch: e.reciprocal(o, i),
          reads=[k("sch")], writes=[k("rs2")])
    kb.op("S", lambda e, o=A, i=x_ap: e.activation(o, i, AF.Abs),
          reads=[xkey], writes=[k("A")])
    Hv = H.rearrange("p (b i) -> p b i", i=16)
    Av = A.rearrange("p (b i) -> p b i", i=16)
    a3, b3 = broadcast_tensor_aps(Av, rs2[:, :, None])
    kb.op("V", lambda e, o=Hv, i=a3, j=b3: e.tensor_tensor(o, i, j, ALU.mult),
          reads=[k("A"), k("rs2")], writes=[k("H")])
    kb.op("V", lambda e, o=q, i=H: e.tensor_scalar(o, i, 0.5, None, ALU.is_gt),
          reads=[k("H")], writes=[k("q")])
    for thr in (1.5, 2.5, 3.5):
        kb.op("V", lambda e, o=q, i=H, a=q, t=thr: e.scalar_tensor_tensor(o, i, t, a, ALU.is_gt, ALU.add),
              reads=[k("H"), k("q")], writes=[k("q")])
    kb.op("V", lambda e, o=m, i=H: e.tensor_scalar(o, i, 5.0, None, ALU.is_gt),
          reads=[k("H")], writes=[k("m")])
    kb.op("V", lambda e, o=m, i=H, a=m: e.scalar_tensor_tensor(o, i, 7.0, a, ALU.is_gt, ALU.add),
          reads=[k("H"), k("m")], writes=[k("m")])
    kb.op("V", lambda e, o=H, i=H: e.tensor_scalar(o, i, 10.0, 2.0, ALU.is_gt, ALU.mult),
          reads=[k("H")], writes=[k("H")])
    kb.op("V", lambda e, o=m, i=H, a=m: e.tensor_tensor(o, i, a, ALU.add),
          reads=[k("H"), k("m")], writes=[k("m")])
    kb.op("V", lambda e, o=q, i=m, a=q: e.scalar_tensor_tensor(o, i, 2.0, a, ALU.mult, ALU.add),
          reads=[k("m"), k("q")], writes=[k("q")])
    kb.op("S", lambda e, o=A, i=x_ap: e.activation(o, i, AF.Sign),
          reads=[xkey, k("A")], writes=[k("A")])
    kb.op("V", lambda e, o=q, i=q, s=A: e.tensor_tensor(o, i, s, ALU.mult),
          reads=[k("q"), k("A")], writes=[k("q")])
    qv = q.rearrange("p (b i) -> p b i", i=16)
    ov = out_bf_ap.rearrange("p (b i) -> p b i", i=16)
    q3, s3 = broadcast_tensor_aps(qv, sch[:, :, None])
    kb.op("V", lambda e, o=ov, i=q3, j=s3: e.tensor_tensor(o, i, j, ALU.mult),
          reads=[k("q"), k("sch")], writes=[outkey])



def build_l0(ctx, io, L, kind_out="Internal"):
    nc, kb = ctx.nc, ctx.kb
    nmt = L // 128
    CH = 256
    nch = L // CH
    ctx.ident_bf = ctx.carve("bf", 128); ib = ctx.ident_bf
    kb.dma("Q", ib, io["ident_bf"], writes=["c_ident"])
    ctx.ident_f32 = ctx.carve("f32", 128)
    kb.dma("Q", ctx.ident_f32, io["ident_f"], writes=["c_identf"])
    ctx.ones_f = ctx.carve("f32", 128)[0:1, :]
    kb.dma("Q", ctx.ones_f, io["ones_f"], writes=["c_ones"])
    triu = ctx.carve("f32", 128)
    ctx.triu_f = triu
    kb.dma("Q", triu, io["triu_f"], writes=["c_triu"])
    iota_b = ctx.carve("f32", 256)
    kb.dma("Q", iota_b, io["iota_b"], writes=["c_iota"])
    phpi = ctx.carve("f32", 1)
    kb.dma("Q", phpi, io["phpi"], writes=["c_phpi"])
    ddup2 = ctx.carve("bf", 128)
    kb.dma("Q", ddup2[0:64, :], io["ddup"], writes=["c_ddup"])
    ddup = ddup2[0:64, :]

    nmt_ = nmt
    mean0 = ctx.carve("f32", nmt); rstd0 = ctx.carve("f32", nmt)
    tmpn = ctx.carve("f32", nmt)
    aoms = ctx.carve("f32", nmt); aoss = ctx.carve("f32", nmt)
    aomean = ctx.carve("f32", nmt); aorstd = ctx.carve("f32", nmt)
    ctx.cst(float(EPS)); ctx.cst(-PI)
    thcol = ctx.carve("f32", NH)
    kb.dma("Q", thcol, io["th_pre"], writes=[("thcol", 0), ("thcol", 1)])
    ctx.mark("L0A", "bf"); ctx.mark("L0A", "f32")

    wst = ctx.carve("f32", 1024)
    W = {}
    for wname in ("wq", "wk"):
        W[wname] = [ctx.carve("bf", 1024) for _ in range(8)]
    _wv0 = ctx.carve("bf", 1024)
    _wv1 = ctx.carve("bf", 1024)
    W["wv"] = [_wv0, _wv1]
    _wo0 = ctx.carve("bf", 1024)
    W["wo"] = [_wo0, _wo0]
    W["g1"] = [ctx.carve("bf", 64) for _ in range(8)]
    g2_bf = ctx.carve("bf", 1024)[0:64, :]
    for wname in ("wq", "wk", "g1"):
        ncol = 64 if wname == "g1" else 1024
        for kt in range(8):
            kb.dma("Q", wst[:, 0:ncol], io[wname][kt * 128:(kt + 1) * 128, :],
                   writes=["wst"])
            kb.op("V", lambda e, o=W[wname][kt], i=wst[:, 0:ncol]: e.tensor_copy(o, i),
                  reads=["wst"], writes=[("w", wname, kt)])
    for kt in range(8):
        kb.dma("Q", wst, io["wo"][kt * 128:(kt + 1) * 128, :], writes=["wst"])
        kb.op("V", lambda e, o=W["wo"][0], i=wst: e.tensor_copy(o, i),
              reads=["wst"], writes=["wopan"])
        kb.dma("P", io["wo_bf_s"][kt * 128:(kt + 1) * 128, :], W["wo"][0],
               reads=["wopan"], writes=[("wo_s", kt)])
    for kt in range(8):
        kb.dma("Q", wst, io["wv"][kt * 128:(kt + 1) * 128, :], writes=["wst"])
        kb.op("V", lambda e, o=_wv0, i=wst: e.tensor_copy(o, i),
              reads=["wst"], writes=[("wvpan", 0)])
        kb.dma("P", io["wv_bf_s"][kt * 128:(kt + 1) * 128, :], _wv0,
               reads=[("wvpan", 0)], writes=[("wv_s", kt)])
    kb.dma("Q", wst[0:64, :], io["g2"], writes=["wst"])
    kb.op("V", lambda e, o=g2_bf, i=wst[0:64, :]: e.tensor_copy(o, i),
          reads=["wst"], writes=[("w", "g2", 0)])

    xs = ctx.carve("f32", 1024)
    sqs = ctx.carve("f32", 1024)
    msum = ctx.carve("f32", nmt); ssum = ctx.carve("f32", nmt)
    for mt in range(nmt):
        kb.dma("Q", xs, io["x"][mt * 128:(mt + 1) * 128, :],
               writes=["xs"])
        emit_ln_stats_tile(ctx, xs, "xs", msum[:, mt:mt + 1], ssum[:, mt:mt + 1],
                           sqs, ("msum", mt), ("ssum", mt), "sqs")
    emit_ln_finish(ctx, msum, ssum, mean0, rstd0, tmpn,
                   (tuple(("msum", mt) for mt in range(nmt)),
                    tuple(("ssum", mt) for mt in range(nmt)), "mean0", "rstd0", "tmp0"))

    x_t = [ctx.carve("f32", 1024) for _ in range(2)]
    xn_f = [ctx.carve("f32", 1024) for _ in range(2)]
    xn_b0 = ctx.carve("bf", 1024)
    xn_b = [xn_b0, xn_b0]
    xnT = [ctx.carve("bf", 256) for _ in range(8)]
    qT = [ctx.carve("bf", 256) for _ in range(16)]
    kT = [ctx.carve("bf", 256) for _ in range(16)]
    v_t = [ctx.carve("bf", 1024) for _ in range(2)]
    t1T = ctx.carve("bf", 256)[0:64, :]
    trig = ctx.carve("f32", 256)
    trig2 = ctx.carve("f32", 256)
    itrig = ctx.carve("i32", 256)
    col2 = ctx.carve("f32", 1)
    qr = ctx.carve("bf", 256)
    kr = ctx.carve("bf", 256)
    E_sb = [ctx.carve("bf", 128) for _ in range(2)]
    ktr_sb = ctx.carve("bf", 128)
    S_sb = ctx.carve("bf", 1024)
    S_acc = ctx.carve("f32", 1024)
    attnT = [ctx.carve("bf", 256) for _ in range(8)]
    gt_s = ctx.carve("f32", 512)
    ao_t = [ctx.carve("f32", 1024) for _ in range(2)]

    PS_S = ctx.ps_S
    PS_a, PS_b, PS_c, PS_d, PS_e = (ctx.ps_f[i] for i in range(5))
    PS_tr = ctx.ps_bf

    def mm(out_ap, lhs_ap, rhs_ap, reads, wkey, start, stop, bank=None):
        kb.mm(lambda e, o=out_ap, l=lhs_ap, r=rhs_ap, st=start, sp=stop:
              e.matmul(o, l, r, start=st, stop=sp), reads=reads, writes=[wkey],
              meta=(bank or wkey, start, stop))

    for ch in range(nch):
        mts = (2 * ch, 2 * ch + 1)
        for j, mt in enumerate(mts):
            kb.dma("Q", x_t[j], io["x"][mt * 128:(mt + 1) * 128, :],
                   writes=[("x_t", j)], reads=[("x_t_free", ch, j)])
            kb.op("V", lambda e, o=xn_f[j], i=x_t[j], m=mean0[:, mt:mt + 1], r=rstd0[:, mt:mt + 1]:
                  e.tensor_scalar(o, i, m, None, ALU.subtract),
                  reads=[("x_t", j), "mean0", "rstd0"], writes=[("xn_f", j)])
            kb.op("V", lambda e, o=xn_f[j], r=rstd0[:, mt:mt + 1]:
                  e.tensor_scalar(o, o, r, None, ALU.mult),
                  reads=[("xn_f", j)], writes=[("xn_f", j)])
            kb.op("V", lambda e: e.engine_nop(), reads=[("xn_f", j)],
                  writes=[("x_t_free", ch + 1, j)])
        for j in range(2):
            kb.op("S", lambda e, o=xn_b[j], i=xn_f[j]: e.copy(o, i),
                  reads=[("xn_f", j)], writes=["xn_b"])
            for kt in range(8):
                transpose128(ctx, xn_b[j][:, kt * 128:(kt + 1) * 128],
                             xnT[kt][:, j * 128:(j + 1) * 128],
                             ["xn_b"], ("xnT", kt))
        for nb in range(8):
            for wn, dstl, dkey in (("wq", qT, "qT"), ("wk", kT, "kT")):
                pp = PS_d if (nb % 2 == 0) else PS_e
                pk = "psf3" if (nb % 2 == 0) else "psf4"
                for kt in range(8):
                    mm(pp.ap()[:, 0:256], W[wn][kt][:, nb * 128:(nb + 1) * 128],
                       xnT[kt][:, 0:256],
                       [("w", wn, kt), ("xnT", kt)], pk, kt == 0, kt == 7)
                kb.op("S", lambda e, o=dstl[2 * nb][0:64, :], i=pp.ap()[0:64, 0:256]:
                      e.activation(o, i, AF.Silu), reads=[pk], writes=[(dkey, 2 * nb)])
                kb.op("S", lambda e, o=dstl[2 * nb + 1][0:64, :], i=pp.ap()[64:128, 0:256]:
                      e.activation(o, i, AF.Silu), reads=[pk], writes=[(dkey, 2 * nb + 1)])
        psmapv = [PS_b, PS_c, PS_d, PS_e]
        for kt in range(8):
            kb.dma("Q", W["wv"][kt % 2], io["wv_bf_s"][kt * 128:(kt + 1) * 128, :],
                   reads=[("wv_s", kt)], writes=[("wvpan", kt % 2)])
            for j in range(2):
                for half in range(2):
                    psn = 2 * j + half
                    mm(psmapv[psn].ap(), xnT[kt][:, j * 128:(j + 1) * 128],
                       W["wv"][kt % 2][:, half * 512:(half + 1) * 512],
                       [("wvpan", kt % 2), ("xnT", kt)], "psf%d" % (psn + 1),
                       kt == 0, kt == 7)
        for j in range(2):
            for half in range(2):
                psn = 2 * j + half
                kb.op("V", lambda e, o=v_t[j][:, half * 512:(half + 1) * 512],
                      i=psmapv[psn].ap(): e.tensor_copy(o, i),
                      reads=["psf%d" % (psn + 1)], writes=[("v_t", j)])
        pp, pk = PS_d, "psf3"
        for kt in range(8):
            mm(pp.ap()[0:64, 0:256], W["g1"][kt], xnT[kt][:, 0:256],
               [("w", "g1", kt), ("xnT", kt)], pk, kt == 0, kt == 7)
        kb.op("V", lambda e, o=t1T, i=pp.ap()[0:64, 0:256]: e.tensor_copy(o, i),
              reads=[pk], writes=["t1T"])

        for hp in range(8):
            for hh in range(2):
                h = 2 * hp + hh
                kb.op("V", lambda e, o=col2, i=thcol[:, h:h + 1], t0=float(ch * CH), p=phpi:
                      e.tensor_scalar(o, i, t0, p, ALU.mult, ALU.add),
                      reads=[("thcol", 0), ("thcol", 1), "c_phpi"], writes=["col2"])
                kb.op("V", lambda e, o=trig, i=iota_b, t=thcol[:, h:h + 1]:
                      e.tensor_scalar(o, i, t, None, ALU.mult),
                      reads=["c_iota", ("thcol", 0), ("thcol", 1)], writes=["trig"])
                kb.op("V", lambda e, o=trig, c=col2:
                      e.tensor_scalar(o, o, c, None, ALU.add),
                      reads=["trig", "col2"], writes=["trig"])
                kb.op("V", lambda e, o=itrig, i=trig: e.tensor_scalar(o, i, 1.0 / (2 * PI), None, ALU.mult),
                      reads=["trig"], writes=["itrig"])
                kb.op("V", lambda e, o=trig, i=itrig, a=trig:
                      e.scalar_tensor_tensor(o, i, -2 * PI, a, ALU.mult, ALU.add),
                      reads=["trig", "itrig"], writes=["trig"])
                kb.op("V", lambda e, o=trig2, i=trig: e.tensor_scalar(o, i, 0.0, 2 * PI, ALU.is_lt, ALU.mult),
                      reads=["trig"], writes=["trig2"])
                kb.op("V", lambda e, o=trig, i=trig, a=trig2: e.tensor_tensor(o, i, a, ALU.add),
                      reads=["trig", "trig2"], writes=["trig"])
                ckm = ctx.cst(-PI)
                kb.op("S", lambda e, o=trig, i=trig: e.activation(o, i, AF.Sin, bias=-PI),
                      reads=["trig", ckm], writes=["trig"])
                for (srcT, dst, dkey) in ((qT, qr, "qr"), (kT, kr, "kr")):
                    mm(PS_c.ap()[:, 0:256], ddup, srcT[h][0:64, :],
                       [("qT" if dst is qr else "kT", h), "c_ddup"], "psf2", True, True)
                    kb.op("V", lambda e, o=dst, i=PS_c.ap()[:, 0:256], t=trig:
                          e.tensor_tensor(o, i, t, ALU.mult),
                          reads=["psf2", "trig"], writes=[dkey])
                if ch > 0:
                    mm(PS_a.ap()[64 * hh:64 * hh + 64, 0:256], S_sb[:, h * 64:h * 64 + 64],
                       qr, ["S_sb", "qr"], "psf0", True, False,
                       bank=("psf0", hh))
                for i in range(2):
                    for jb in range(i + 1):
                        eps = PS_b if (jb % 2 == 0) else PS_c
                        ek = "psf1" if (jb % 2 == 0) else "psf2"
                        mm(eps.ap()[:, 0:128], kr[:, jb * 128:(jb + 1) * 128],
                           qr[:, i * 128:(i + 1) * 128], ["kr", "qr"], ek, True, True)
                        if jb == i:
                            kb.op("V", lambda e, o=E_sb[jb % 2], i=eps.ap()[:, 0:128], m=triu:
                                  e.tensor_tensor(o, i, m, ALU.mult),
                                  reads=[ek, "c_triu"], writes=[("E", jb % 2)])
                        else:
                            kb.op("V", lambda e, o=E_sb[jb % 2], i=eps.ap()[:, 0:128]:
                                  e.tensor_copy(o, i), reads=[ek], writes=[("E", jb % 2)])
                        st = (ch == 0 and i == 0 and jb == 0)
                        sp = (i == 1 and jb == i)
                        mm(PS_a.ap()[64 * hh:64 * hh + 64, i * 128:(i + 1) * 128],
                           v_t[jb][:, 64 * h:64 * h + 64], E_sb[jb % 2],
                           [("v_t", jb), ("E", jb % 2)], "psf0", st, sp,
                           bank=("psf0", hh))
                for jb in range(2):
                    kb.mm(lambda e, o=PS_tr.ap()[:, 0:128], i=kr[:, jb * 128:(jb + 1) * 128],
                          idn=ctx.ident_bf: e.transpose(o, i, idn),
                          reads=["kr", "c_ident"], writes=["psbf"])
                    kb.op("V", lambda e, o=ktr_sb, i=PS_tr.ap()[:, 0:128]: e.tensor_copy(o, i),
                          reads=["psbf"], writes=["ktr"])
                    st = (jb == 0 and h in (0, 8))
                    sp = (jb == 1 and h in (7, 15))
                    mm(PS_S.ap()[:, h * 64:h * 64 + 64], ktr_sb,
                       v_t[jb][:, 64 * h:64 * h + 64], ["ktr", ("v_t", jb)], "psS", st, sp,
                       bank=("psS", h // 8))
            kb.op("V", lambda e, o=attnT[hp], i=PS_a.ap()[:, 0:256]: e.tensor_copy(o, i),
                  reads=["psf0"], writes=[("attnT", hp)])
        if ch == 0:
            kb.op("V", lambda e, o=S_acc, i=PS_S.ap(): e.tensor_copy(o, i),
                  reads=["psS"], writes=["S_acc"])
        else:
            kb.op("V", lambda e, o=S_acc, i=S_acc, d=PS_S.ap(): e.tensor_tensor(o, i, d, ALU.add),
                  reads=["psS", "S_acc"], writes=["S_acc"])
        if ch < nch - 1:
            kb.op("V", lambda e, o=S_sb, i=S_acc: e.tensor_copy(o, i),
                  reads=["S_acc"], writes=["S_sb"])
        psmap = [PS_b, PS_c, PS_d, PS_e]
        for dt_ in range(8):
            kb.dma("Q", W["wo"][dt_ % 2], io["wo_bf_s"][dt_ * 128:(dt_ + 1) * 128, :],
                   reads=[("wo_s", dt_)], writes=["wopan"])
            for j in range(2):
                for half in range(2):
                    psn = 2 * j + half
                    mm(psmap[psn].ap(), attnT[dt_][:, j * 128:(j + 1) * 128],
                       W["wo"][dt_ % 2][:, half * 512:(half + 1) * 512],
                       [("attnT", dt_), "wopan"], "psf%d" % (psn + 1),
                       dt_ == 0, dt_ == 7)
        for j in range(2):
            for half in range(2):
                psn = 2 * j + half
                pk = "psf%d" % (psn + 1)
                mm(PS_a.ap(), t1T[:, j * 128:(j + 1) * 128],
                   g2_bf[:, half * 512:(half + 1) * 512],
                   ["t1T", ("w", "g2", 0)], "psf0", True, True)
                kb.op("S", lambda e, o=gt_s, i=PS_a.ap()[:, 0:512]:
                      e.activation(o, i, AF.Tanh, scale=0.5), reads=["psf0"], writes=["gt_s"])
                kb.op("V", lambda e, o=gt_s, i=gt_s: e.tensor_scalar(o, i, 0.5, 0.5, ALU.mult, ALU.add),
                      reads=["gt_s"], writes=["gt_s"])
                kb.op("V", lambda e, o=ao_t[j][:, half * 512:(half + 1) * 512],
                      i=psmap[psn].ap(), g=gt_s: e.tensor_tensor(o, i, g, ALU.mult),
                      reads=[pk, "gt_s"], writes=[("ao_t", j)])
            mt = mts[j]
            emit_ln_stats_tile(ctx, ao_t[j], ("ao_t", j), aoms[:, mt:mt + 1], aoss[:, mt:mt + 1],
                               sqs, ("aoms", mt), ("aoss", mt), "sqs")
            kb.dma("P", io["ao_s"][mt * 128:(mt + 1) * 128, :], ao_t[j],
                   reads=[("ao_t", j)], writes=[("ao_done", mt)])
    emit_ln_finish(ctx, aoms, aoss, aomean, aorstd, tmpn,
                   (tuple(("aoms", mt) for mt in range(nmt)),
                    tuple(("aoss", mt) for mt in range(nmt)), "aomean", "aorstd", "tmpao"))
    ctx.l0_state = dict(mean0=mean0, rstd0=rstd0, aomean=aomean, aorstd=aorstd,
                        xs=xs, sqs=sqs)


def build_l0b(ctx, io, L):
    nc, kb = ctx.nc, ctx.kb
    kb.phase_flush()
    ctx.reset("L0A", "bf"); ctx.reset("L0A", "f32")
    nmt = L // 128
    st = ctx.l0_state
    mean0, rstd0, aomean, aorstd = st["mean0"], st["rstd0"], st["aomean"], st["aorstd"]
    GRP = min(512, L)
    ngrp = L // GRP
    gm = GRP // 128
    strips = [(i * 512, min(512, DI - i * 512)) for i in range(cdiv(DI, 512))]

    grow = ctx.carve("f32", 1024)[0:1, :]
    brow = ctx.carve("f32", 1024)[0:1, :]
    gamma_bc = ctx.carve("f32", 1024)
    beta_bc = ctx.carve("f32", 1024)
    kb.dma("Q", grow, io["gamma"][None, :], writes=["grow"])
    kb.dma("Q", brow, io["beta"][None, :], writes=["brow"])
    bcast_row_f32(ctx, grow, gamma_bc, ["grow"], "gamma_bc", 0)
    bcast_row_f32(ctx, brow, beta_bc, ["brow"], "beta_bc", 1)

    CW = 1408
    wstage = ctx.carve("f32", CW)
    qtmp = {"bm": ctx.carve("f32", CW // 16), "rs2": ctx.carve("f32", CW // 16),
            "sch": ctx.carve("f32", CW // 16), "A": ctx.carve("f32", CW),
            "H": ctx.carve("f32", CW), "q": ctx.carve("f32", CW), "m": ctx.carve("f32", CW)}
    wdq = ctx.carve("bf", CW)
    trslot = ctx.carve("bf", 128)
    for wname, src_d, dst_d, nrow, ncol in (
            ("wg", io["wg"], io["wgT_s"], DI, D),
            ("wu", io["wu"], io["wuT_s"], DI, D),
            ("wd", io["wd"], io["wdT_s"], D, DI)):
        for rt in range(nrow // 128):
            for c0 in range(0, ncol, CW):
                cw = min(CW, ncol - c0)
                kb.dma("Q", wstage[:, 0:cw], src_d[rt * 128:(rt + 1) * 128, c0:c0 + cw],
                       writes=["wstage"])
                emit_quant_fp4(ctx, wstage[:, 0:cw], "wstage", wdq[:, 0:cw], "wdq",
                               cw, {k: (v[:, 0:cw // 16] if k in ("bm", "rs2", "sch") else v[:, 0:cw])
                                    for k, v in qtmp.items()}, tag="wq")
                for ct in range(cw // 128):
                    ctg = (c0 + ct * 128) // 128
                    kb.mm(lambda e, o=ctx.ps_bf.ap()[:, 0:128], i=wdq[:, ct * 128:(ct + 1) * 128],
                          idn=ctx.ident_bf: e.transpose(o, i, idn),
                          reads=["wdq", "c_ident"], writes=["psbf"])
                    kb.op("V", lambda e, o=trslot, i=ctx.ps_bf.ap()[:, 0:128]: e.tensor_copy(o, i),
                          reads=["psbf"], writes=["trslot"])
                    kb.dma("P", dst_d[ctg * 128:(ctg + 1) * 128, rt * 128:(rt + 1) * 128], trslot,
                           reads=["trslot"], writes=[("wT_done", wname, rt, ctg)])
        kb.barrier_region(("wT_all", wname), "V",
                          [("wT_done", wname, rt, ctg) for rt in range(nrow // 128)
                           for ctg in range(ncol // 128)])

    xdeqT = [ctx.carve("bf", GRP) for _ in range(8)]
    hdeqT = [ctx.carve("bf", GRP) for _ in range(cdiv(DI, 128))]
    xq_t = ctx.carve("f32", 1024)
    xn_t = ctx.carve("f32", 1024)
    xdq_b = ctx.carve("bf", 1024)
    pan_g = [ctx.carve("bf", 512) for _ in range(8)]
    pan_u = [ctx.carve("bf", 512) for _ in range(8)]
    gs_s = ctx.carve("f32", 512)
    h_s = ctx.carve("f32", 512)
    hdq_b = ctx.carve("bf", 512)
    pan_d = [ctx.carve("bf", 1024) for _ in range(2)]
    x_t2 = ctx.carve("f32", 1024)
    ao_t2 = ctx.carve("f32", 1024)
    aoln = ctx.carve("f32", 1024)
    x1_t = ctx.carve("f32", 1024)
    qtmp2 = {"bm": ctx.carve("f32", 64), "rs2": ctx.carve("f32", 64),
             "sch": ctx.carve("f32", 64), "A": ctx.carve("f32", 1024),
             "H": ctx.carve("f32", 1024), "q": ctx.carve("f32", 1024),
             "m": ctx.carve("f32", 1024)}
    PS_g, PS_u, PS_d1, PS_d2, PS_t = (ctx.ps_f[i] for i in range(5))

    for grp in range(ngrp):
        for j in range(gm):
            mt = grp * gm + j
            kb.dma("Q", xq_t, io["x"][mt * 128:(mt + 1) * 128, :], writes=["xq_t"])
            kb.op("V", lambda e, o=xn_t, i=xq_t, m=mean0[:, mt:mt + 1], r=rstd0[:, mt:mt + 1]:
                  e.tensor_scalar(o, i, m, None, ALU.subtract),
                  reads=["xq_t", "mean0", "rstd0"], writes=["xn_t"])
            kb.op("V", lambda e, o=xn_t, r=rstd0[:, mt:mt + 1]:
                  e.tensor_scalar(o, o, r, None, ALU.mult),
                  reads=["xn_t"], writes=["xn_t"])
            emit_quant_fp4(ctx, xn_t, "xn_t", xdq_b, "xdq_b", 1024, qtmp2, tag="q2")
            for kt in range(8):
                kb.mm(lambda e, o=ctx.ps_bf.ap()[:, 0:128], i=xdq_b[:, kt * 128:(kt + 1) * 128],
                      idn=ctx.ident_bf: e.transpose(o, i, idn),
                      reads=["xdq_b", "c_ident"], writes=["psbf"])
                kb.op("V", lambda e, o=xdeqT[kt][:, j * 128:(j + 1) * 128],
                      i=ctx.ps_bf.ap()[:, 0:128]: e.tensor_copy(o, i),
                      reads=["psbf"], writes=[("xdeqT", kt)])
        for s_i, (s0, sw) in enumerate(strips):
            for kt in range(8):
                kb.dma("Q", pan_g[kt][:, 0:sw], io["wgT_s"][kt * 128:(kt + 1) * 128, s0:s0 + sw],
                       reads=[("wT_all", "wg")], writes=[("pan_g", kt)])
                kb.dma("Q", pan_u[kt][:, 0:sw], io["wuT_s"][kt * 128:(kt + 1) * 128, s0:s0 + sw],
                       reads=[("wT_all", "wu")], writes=[("pan_u", kt)])
            for j in range(gm):
                for kt in range(8):
                    kb.mm(lambda e, o=PS_g.ap()[:, 0:sw], l=xdeqT[kt][:, j * 128:(j + 1) * 128],
                          r=pan_g[kt][:, 0:sw], st=(kt == 0), sp=(kt == 7):
                          e.matmul(o, l, r, start=st, stop=sp),
                          reads=[("xdeqT", kt), ("pan_g", kt)], writes=["psf0"],
                          meta=("psf0", kt == 0, kt == 7))
                    kb.mm(lambda e, o=PS_u.ap()[:, 0:sw], l=xdeqT[kt][:, j * 128:(j + 1) * 128],
                          r=pan_u[kt][:, 0:sw], st=(kt == 0), sp=(kt == 7):
                          e.matmul(o, l, r, start=st, stop=sp),
                          reads=[("xdeqT", kt), ("pan_u", kt)], writes=["psf1"],
                          meta=("psf1", kt == 0, kt == 7))
                kb.op("S", lambda e, o=gs_s[:, 0:sw], i=PS_g.ap()[:, 0:sw]:
                      e.activation(o, i, AF.Silu), reads=["psf0"], writes=["gs_s"])
                kb.op("V", lambda e, o=h_s[:, 0:sw], i=gs_s[:, 0:sw], u=PS_u.ap()[:, 0:sw]:
                      e.tensor_tensor(o, i, u, ALU.mult), reads=["gs_s", "psf1"], writes=["h_s"])
                emit_quant_fp4(ctx, h_s[:, 0:sw], "h_s", hdq_b[:, 0:sw], "hdq_b", sw,
                               {k: (v[:, 0:sw // 16] if k in ("bm", "rs2", "sch") else v[:, 0:sw])
                                for k, v in qtmp2.items()}, tag="q2")
                for ct in range(sw // 128):
                    kt_global = (s0 + ct * 128) // 128
                    kb.mm(lambda e, o=ctx.ps_bf.ap()[:, 0:128], i=hdq_b[:, ct * 128:(ct + 1) * 128],
                          idn=ctx.ident_bf: e.transpose(o, i, idn),
                          reads=["hdq_b", "c_ident"], writes=["psbf"])
                    kb.op("V", lambda e, o=hdeqT[kt_global][:, j * 128:(j + 1) * 128],
                          i=ctx.ps_bf.ap()[:, 0:128]: e.tensor_copy(o, i),
                          reads=["psbf"], writes=[("hdeqT", kt_global)])
        nkt_d = DI // 128
        dn_banks = {}
        for j in range(gm):
            for half in range(2):
                dn_banks[(j, half)] = (ctx.ps_f[(2 * j + half) % 5] if False else None)
        for jp in range(gm // 2):
            for kt in range(nkt_d):
                pd = pan_d[kt % 2]
                kb.dma("Q", pd, io["wdT_s"][kt * 128:(kt + 1) * 128, :],
                       reads=[("wT_all", "wd")], writes=[("pan_d", kt % 2)])
                for jj in range(2):
                    j = jp * 2 + jj
                    for half in range(2):
                        psn = 2 * jj + half
                        kb.mm(lambda e, o=ctx.ps_f[psn].ap(), l=hdeqT[kt][:, j * 128:(j + 1) * 128],
                              r=pd[:, half * 512:(half + 1) * 512], st=(kt == 0), sp=(kt == nkt_d - 1):
                              e.matmul(o, l, r, start=st, stop=sp),
                              reads=[("hdeqT", kt), ("pan_d", kt % 2)], writes=["psf%d" % psn],
                              meta=("psf%d" % psn, kt == 0, kt == nkt_d - 1))
            for jj in range(2):
                j = jp * 2 + jj
                mt = grp * gm + j
                kb.dma("Q", x_t2, io["x"][mt * 128:(mt + 1) * 128, :], writes=["x_t2"])
                kb.dma("Q", ao_t2, io["ao_s"][mt * 128:(mt + 1) * 128, :],
                       reads=[("ao_done", mt)], writes=["ao_t2"])
                kb.op("V", lambda e, o=aoln, i=ao_t2, m=aomean[:, mt:mt + 1], r=aorstd[:, mt:mt + 1]:
                      e.tensor_scalar(o, i, m, None, ALU.subtract),
                      reads=["ao_t2", "aomean", "aorstd"], writes=["aoln"])
                kb.op("V", lambda e, o=aoln, r=aorstd[:, mt:mt + 1]:
                      e.tensor_scalar(o, o, r, None, ALU.mult),
                      reads=["aoln"], writes=["aoln"])
                kb.op("V", lambda e, o=aoln, i=aoln, g=gamma_bc: e.tensor_tensor(o, i, g, ALU.mult),
                      reads=["aoln", "gamma_bc"], writes=["aoln"])
                kb.op("V", lambda e, o=aoln, i=aoln, b=beta_bc: e.tensor_tensor(o, i, b, ALU.add),
                      reads=["aoln", "beta_bc"], writes=["aoln"])
                kb.op("V", lambda e, o=x1_t, i=x_t2, a=aoln: e.tensor_tensor(o, i, a, ALU.add),
                      reads=["x_t2", "aoln"], writes=["x1_t"])
                for half in range(2):
                    psn = 2 * jj + half
                    kb.op("V", lambda e, o=x1_t[:, half * 512:(half + 1) * 512],
                          i=x1_t[:, half * 512:(half + 1) * 512], d=ctx.ps_f[psn].ap():
                          e.tensor_tensor(o, i, d, ALU.add),
                          reads=["x1_t", "psf%d" % psn], writes=["x1_t"])
                kb.dma("P", io["x1"][mt * 128:(mt + 1) * 128, :], x1_t,
                       reads=["x1_t"], writes=[("x1_done", mt)])



def _row_ops(ctx):
    kb = ctx.kb
    def tt(out, a, b, op, rk, wk):
        kb.op("V", lambda e, o=out, x=a, y=b, p=op: e.tensor_tensor(o, x, y, p),
              reads=rk, writes=wk)
    def ts(out, a, s1, s2, op0, op1, rk, wk):
        kb.op("V", lambda e, o=out, x=a, u=s1, v=s2, p0=op0, p1=op1:
              e.tensor_scalar(o, x, u, v, p0, p1), reads=rk, writes=wk)
    def act(out, a, f, rk, wk, bias=0.0, scale=1.0):
        extra = []
        if isinstance(bias, float) and bias != 0.0:
            extra = [ctx.cst(bias)]
        kb.op("S", lambda e, o=out, x=a, ff=f, b=bias, s=scale:
              e.activation(o, x, ff, bias=b, scale=s), reads=list(rk) + extra, writes=wk)
    return tt, ts, act


def emit_l1_stats(ctx, io, L):
    nc, kb = ctx.nc, ctx.kb
    nmt = L // 128
    mean1 = ctx.carve("f32", nmt); rstd1 = ctx.carve("f32", nmt)
    tmpn = ctx.carve("f32", nmt)
    xmaxc = ctx.carve("f32", 1)
    ctx.l1 = dict(mean1=mean1, rstd1=rstd1, xmaxc=xmaxc)
    xs = ctx.carve("f32", 1024)
    sqs = ctx.carve("f32", 1024)
    msum = ctx.carve("f32", nmt); ssum = ctx.carve("f32", nmt)
    for mt in range(nmt):
        kb.dma("Q", xs, io["x1"][mt * 128:(mt + 1) * 128, :], writes=["xs1"],
               reads=[("x1_done", mt)])
        emit_ln_stats_tile(ctx, xs, "xs1", msum[:, mt:mt + 1], ssum[:, mt:mt + 1],
                           sqs, ("m1", mt), ("s1", mt), "sq1")
    emit_ln_finish(ctx, msum, ssum, mean1, rstd1, tmpn,
                   (tuple(("m1", mt) for mt in range(nmt)),
                    tuple(("s1", mt) for mt in range(nmt)), "mean1", "rstd1", "tmp1"))


def build_l1_xmax(ctx, io):
    kb = ctx.kb
    xmrow = ctx.carve("f32", 128)[0:1, :]
    _colmax_to_scalar(ctx, ctx.l1["xmaxc"], "xmax", xmrow[:, 0:1], "xmrow", None)
    kb.dma("P", io["xmax_out"], xmrow[:, 0:1], reads=["xmrow"], writes=["xmax_od"])


def build_l1_scan(ctx, io, L, out_stats=False):
    nc, kb = ctx.nc, ctx.kb
    kb.phase_flush()
    ctx.reset_if("L0A", "bf"); ctx.reset_if("L0A", "f32")
    tt, ts, act = _row_ops(ctx)
    nmt = L // 128
    nch = L // 128
    P2 = 512

    emit_l1_stats(ctx, io, L)
    mean1, rstd1, xmaxc = ctx.l1["mean1"], ctx.l1["rstd1"], ctx.l1["xmaxc"]
    ctx.mark("L1Y", "bf"); ctx.mark("L1Y", "f32")
    ys = [ctx.carve("bf", 512) for _ in range(L // 128)]
    ctx.l1ys = ys
    ctx.l1nch = L // 128
    ctx.mark("L1S", "bf"); ctx.mark("L1S", "f32")

    TU1 = ctx.carve("f32", 512); TU2 = ctx.carve("f32", 512)
    TV1 = ctx.carve("f32", 512); TV2 = ctx.carve("f32", 512)
    CU1 = ctx.carve("f32", 512); CU2 = ctx.carve("f32", 512)
    CV1 = ctx.carve("f32", 512); CV2 = ctx.carve("f32", 512)
    A21c = ctx.carve("f32", 512); A22c = ctx.carve("f32", 512)
    EL1 = ctx.carve("f32", 512); EL2 = ctx.carve("f32", 512)
    ctx.l1a128 = [ctx.carve("f32", 512)[0:1, :] for _ in range(4)]
    ctx.mark("L1T", "f32"); ctx.mark("L1T", "bf")
    R = {}
    for name in ("steps_raw", "G_raw", "A_raw"):
        R[name] = ctx.carve("f32", 256)[0:1, :]
        kb.dma("Q", R[name], io[name][None, :], writes=[("prow", name)])
    r_ = lambda: ctx.carve("f32", 256)[0:1, :]
    steps, G, A = r_(), r_(), r_()
    S1, rS1, alpha = r_(), r_(), r_()
    m11, m12, m21, m22 = r_(), r_(), r_(), r_()
    rc, lnS, rsv, rrs = r_(), r_(), r_(), r_()
    p11, p12, p21, p22 = r_(), r_(), r_(), r_()
    theta_r, tmpr, tmpr2 = r_(), r_(), r_()
    ALUm = ALU
    act(steps, R["steps_raw"], AF.Sigmoid, [("prow", "steps_raw")], ["steps"])
    ts(G, R["G_raw"], 0.0, None, ALUm.max, ALUm.bypass, [("prow", "G_raw")], ["G"])
    ts(A, R["A_raw"], 0.0, None, ALUm.max, ALUm.bypass, [("prow", "A_raw")], ["A"])
    tt(S1, steps, G, ALUm.mult, ["steps", "G"], ["S1"])
    ts(S1, S1, 1.0, None, ALUm.add, ALUm.bypass, ["S1"], ["S1"])
    kb.op("V", lambda e, o=rS1, i=S1: e.reciprocal(o, i), reads=["S1"], writes=["rS1"])
    tt(alpha, steps, steps, ALUm.mult, ["steps"], ["alpha"])
    tt(alpha, alpha, A, ALUm.mult, ["alpha", "A"], ["alpha"])
    tt(alpha, alpha, rS1, ALUm.mult, ["alpha", "rS1"], ["alpha"])
    act(alpha, alpha, AF.Tanh, ["alpha"], ["alpha"], scale=1.0 / 1.99)
    ts(alpha, alpha, 1.99, None, ALUm.mult, ALUm.bypass, ["alpha"], ["alpha"])
    t_copy(kb, "V", m11, rS1, ["rS1"], ["m11"])
    kb.op("V", lambda e, o=tmpr, i=steps: e.reciprocal(o, i), reads=["steps"], writes=["rstp"])
    tt(m12, alpha, tmpr, ALUm.mult, ["alpha", "rstp"], ["m12"])
    ts(m12, m12, -1.0, None, ALUm.mult, ALUm.bypass, ["m12"], ["m12"])
    tt(m21, steps, rS1, ALUm.mult, ["steps", "rS1"], ["m21"])
    ts(m22, alpha, -1.0, 1.0, ALUm.mult, ALUm.add, ["alpha"], ["m22"])
    tt(rc, m11, m22, ALUm.add, ["m11", "m22"], ["rc"])
    ts(rc, rc, 0.5, None, ALUm.mult, ALUm.bypass, ["rc"], ["rc"])
    act(lnS, S1, AF.Ln, ["S1"], ["lnS"])
    tt(tmpr, rc, rc, ALUm.mult, ["rc"], ["rc2"])
    tt(rsv, m11, tmpr, ALUm.subtract, ["m11", "rc2"], ["rsv"])
    ts(rsv, rsv, 1e-24, None, ALUm.max, ALUm.bypass, ["rsv"], ["rsv"])
    act(rsv, rsv, AF.Ln, ["rsv"], ["rsv"])
    act(rsv, rsv, AF.Exp, ["rsv"], ["rsv"], scale=0.5)
    kb.op("V", lambda e, o=rrs, i=rsv: e.reciprocal(o, i), reads=["rsv"], writes=["rrs"])
    tt(p11, m11, rc, ALUm.subtract, ["m11", "rc"], ["p11"])
    tt(p11, p11, rrs, ALUm.mult, ["p11", "rrs"], ["p11"])
    tt(p12, m12, rrs, ALUm.mult, ["m12", "rrs"], ["p12"])
    tt(p21, m21, rrs, ALUm.mult, ["m21", "rrs"], ["p21"])
    tt(p22, m22, rc, ALUm.subtract, ["m22", "rc"], ["p22"])
    tt(p22, p22, rrs, ALUm.mult, ["p22", "rrs"], ["p22"])
    ab_, mx_, mn_, phi_, m1_, m2_, sg_, bs_ = (r_() for _ in range(8))
    act(ab_, rc, AF.Abs, ["rc"], ["ab_"])
    tt(mx_, rsv, ab_, ALUm.max, ["rsv", "ab_"], ["mx_"])
    tt(mn_, rsv, ab_, ALUm.min, ["rsv", "ab_"], ["mn_"])
    kb.op("V", lambda e, o=tmpr, i=mx_: e.reciprocal(o, i), reads=["mx_"], writes=["rmx_"])
    tt(phi_, mn_, tmpr, ALUm.mult, ["mn_", "rmx_"], ["phi_"])
    act(phi_, phi_, AF.Arctan, ["phi_"], ["phi_"])
    ts(m1_, rc, 0.0, None, ALUm.is_lt, ALUm.bypass, ["rc"], ["m1_"])
    tt(m2_, rsv, ab_, ALUm.is_gt, ["rsv", "ab_"], ["m2_"])
    tt(sg_, m1_, m2_, ALUm.subtract, ["m1_", "m2_"], ["sg_"])
    tt(sg_, sg_, sg_, ALUm.mult, ["sg_"], ["sg_"])
    ts(sg_, sg_, -2.0, 1.0, ALUm.mult, ALUm.add, ["sg_"], ["sg_"])
    ts(bs_, m2_, 0.5 * PI, None, ALUm.mult, ALUm.bypass, ["m2_"], ["bs_"])
    kb.op("V", lambda e, o=bs_, i=m1_, t=bs_: e.scalar_tensor_tensor(o, i, PI, t, ALUm.mult, ALUm.add),
          reads=["m1_", "bs_"], writes=["bs_"])
    tt(tmpr2, m1_, m2_, ALUm.mult, ["m1_", "m2_"], ["m12_"])
    kb.op("V", lambda e, o=bs_, i=tmpr2, t=bs_: e.scalar_tensor_tensor(o, i, -PI, t, ALUm.mult, ALUm.add),
          reads=["m12_", "bs_"], writes=["bs_"])
    tt(theta_r, sg_, phi_, ALUm.mult, ["sg_", "phi_"], ["theta_r"])
    tt(theta_r, theta_r, bs_, ALUm.add, ["theta_r", "bs_"], ["theta_r"])
    c1r, c2r = r_(), r_()
    t_copy(kb, "V", c1r, m21, ["m21"], ["c1r"])
    tt(c2r, steps, m21, ALUm.mult, ["steps", "m21"], ["c2r"])

    def dup_row(src, skey, name):
        d = ctx.carve("f32", P2)[0:1, :]
        kb.op("V", lambda e, o=d[:, 0::2], i=src: e.tensor_copy(o, i),
              reads=[skey], writes=[name])
        kb.op("V", lambda e, o=d[:, 1::2], i=src: e.tensor_copy(o, i),
              reads=[skey], writes=[name])
        return d
    thD = dup_row(theta_r, "theta_r", "thD")
    lnrD = dup_row(lnS, "lnS", "lnrD")
    c1D = dup_row(c1r, "c1r", "c1D")
    c2D = dup_row(c2r, "c2r", "c2D")
    p11D = dup_row(p11, "p11", "p11D")
    p12D = dup_row(p12, "p12", "p12D")
    p21D = dup_row(p21, "p21", "p21D")
    p22D = dup_row(p22, "p22", "p22D")

    th_bc = ctx.carve("f32", P2); lnr_bc = ctx.carve("f32", P2)
    bcast_row_f32(ctx, thD, th_bc, ["thD"], "th_bc", 0)
    bcast_row_f32(ctx, lnrD, lnr_bc, ["lnrD"], "lnr_bc", 1)
    c1_bc = ctx.carve("f32", P2); c2_bc = ctx.carve("f32", P2)
    bcast_row_f32(ctx, c1D, c1_bc, ["c1D"], "c1_bc", 0)
    bcast_row_f32(ctx, c2D, c2_bc, ["c2D"], "c2_bc", 1)
    p11_bc = ctx.carve("f32", P2); p12_bc = ctx.carve("f32", P2)
    p21_bc = ctx.carve("f32", P2); p22_bc = ctx.carve("f32", P2)
    bcast_row_f32(ctx, p11D, p11_bc, ["p11D"], "p11_bc", 0)
    bcast_row_f32(ctx, p12D, p12_bc, ["p12D"], "p12_bc", 1)
    bcast_row_f32(ctx, p21D, p21_bc, ["p21D"], "p21_bc", 0)
    bcast_row_f32(ctx, p22D, p22_bc, ["p22D"], "p22_bc", 1)

    _trig_scratch = [ctx.carve("f32", 512) for _ in range(5)]
    _trig_scratch_w = [ctx.carve("f32", 512)]
    ctx.trig_i32 = ctx.carve("i32", 512)
    iota_c = ctx.carve("f32", 1)
    kb.dma("Q", iota_c, io["iota_col"], writes=["iota_c"])
    riota_c = ctx.carve("f32", 1)
    kb.dma("Q", riota_c, io["riota_col"], writes=["riota_c"])

    def trig_tables(iot, ikey, tag):
        cos_t, sin_t, rp_m, rp_p, ang = _trig_scratch
        k = lambda n: ("trig", n)
        kb.op("V", lambda e, o=ang, i=th_bc, s=iot: e.tensor_scalar(o, i, s, None, ALU.mult),
              reads=["th_bc", ikey], writes=[k("ang")])
        itr = ctx.trig_i32
        ckm = ctx.cst(-PI)
        _trw = _trig_scratch_w[0]
        def _red_sin(dst, phase, dkey):
            if phase != 0.0:
                kb.op("V", lambda e, o=dst, i=ang, p=phase: e.tensor_scalar(o, i, p, None, ALU.add),
                      reads=[k("ang")], writes=[k(dkey)])
                srck = [k(dkey)]
            else:
                kb.op("V", lambda e, o=dst, i=ang: e.tensor_copy(o, i),
                      reads=[k("ang")], writes=[k(dkey)])
                srck = [k(dkey)]
            kb.op("V", lambda e, o=itr, i=dst: e.tensor_scalar(o, i, 1.0 / (2 * PI), None, ALU.mult),
                  reads=srck, writes=[k("itr")])
            kb.op("V", lambda e, o=dst, i=itr, a=dst:
                  e.scalar_tensor_tensor(o, i, -2 * PI, a, ALU.mult, ALU.add),
                  reads=srck + [k("itr")], writes=[k(dkey)])
            kb.op("V", lambda e, o=_trw, i=dst: e.tensor_scalar(o, i, 0.0, 2 * PI, ALU.is_lt, ALU.mult),
                  reads=[k(dkey)], writes=[k("trw")])
            kb.op("V", lambda e, o=dst, i=dst, a=_trw: e.tensor_tensor(o, i, a, ALU.add),
                  reads=[k(dkey), k("trw")], writes=[k(dkey)])
            kb.op("S", lambda e, o=dst, i=dst: e.activation(o, i, AF.Sin, bias=-PI),
                  reads=[k(dkey), ckm], writes=[k(dkey)])
            kb.op("V", lambda e, o=dst, i=dst: e.tensor_scalar(o, i, -1.0, None, ALU.mult),
                  reads=[k(dkey)], writes=[k(dkey)])
        _red_sin(cos_t, 0.5 * PI, "cos")
        _red_sin(sin_t, 0.0, "sin")
        kb.op("V", lambda e, o=ang, i=lnr_bc, s=iot: e.tensor_scalar(o, i, s, 0.5, ALU.mult, ALU.mult),
              reads=["lnr_bc", ikey], writes=[k("ang"), k("slnr")])
        kb.op("S", lambda e, o=rp_m, i=ang: e.activation(o, i, AF.Exp),
              reads=[k("slnr")], writes=[k("rp_m")])
        kb.op("S", lambda e, o=rp_p, i=ang: e.activation(o, i, AF.Exp, scale=-1.0),
              reads=[k("slnr")], writes=[k("rp_p")])
        return cos_t, sin_t, rp_m, rp_p, k

    cosS, sinS, rpm, rpp, kS = trig_tables(iota_c, "iota_c", "tS")
    tt(TU1, rpm, cosS, ALU.mult, [kS("rp_m"), kS("cos")], ["TU1"])
    tt(TU2, TU1, c2_bc, ALU.mult, ["TU1", "c2_bc"], ["TU2"])
    tt(TU1, TU1, c1_bc, ALU.mult, ["TU1", "c1_bc"], ["TU1"])
    tt(TV1, rpm, sinS, ALU.mult, [kS("rp_m"), kS("sin")], ["TV1"])
    tt(TV2, TV1, c2_bc, ALU.mult, ["TV1", "c2_bc"], ["TV2"])
    tt(TV1, TV1, c1_bc, ALU.mult, ["TV1", "c1_bc"], ["TV1"])
    tt(CU1, rpp, sinS, ALU.mult, [kS("rp_p"), kS("sin")], ["CU1"])
    tt(CV1, rpp, cosS, ALU.mult, [kS("rp_p"), kS("cos")], ["CV1"])
    tt(CU2, CU1, p22_bc, ALU.mult, ["CU1", "p22_bc"], ["CU2"])
    tt(CU2, CU2, CV1, ALU.add, ["CU2", "CV1"], ["CU2"])
    tt(CV2, CV1, p22_bc, ALU.mult, ["CV1", "p22_bc"], ["CV2"])
    tt(CV2, CU1, CV2, ALU.subtract, ["CU1", "CV2"], ["CV2"])
    tt(CU1, CU1, p21_bc, ALU.mult, ["CU1", "p21_bc"], ["CU1"])
    tt(CV1, CV1, p21_bc, ALU.mult, ["CV1", "p21_bc"], ["CV1"])
    ts(CV1, CV1, -1.0, None, ALU.mult, ALU.bypass, ["CV1"], ["CV1"])
    iotp_c = ctx.carve("f32", 1)
    kb.dma("Q", iotp_c, io["iotap_col"], writes=["iotp_c"])
    cosP, sinP, _, rpP, kP = trig_tables(iotp_c, "iotp_c", "tP")
    tt(A21c, rpP, sinP, ALU.mult, [kP("rp_p"), kP("sin")], ["A21c"])
    tt(EL1, rpP, cosP, ALU.mult, [kP("rp_p"), kP("cos")], ["EL1"])
    tt(A22c, A21c, p22_bc, ALU.mult, ["A21c", "p22_bc"], ["A22c"])
    tt(A22c, A22c, EL1, ALU.add, ["A22c", "EL1"], ["A22c"])
    tt(A21c, A21c, p21_bc, ALU.mult, ["A21c", "p21_bc"], ["A21c"])
    cosR, sinR, _, rpR, kR = trig_tables(riota_c, "riota_c", "tR")
    rS_ = _trig_scratch[4]
    rC_ = _trig_scratch[2]
    kb.op("V", lambda e, o=rS_, i=rpR, j=sinR: e.tensor_tensor(o, i, j, ALU.mult),
          reads=[kR("rp_p"), kR("sin")],
          writes=[("trig", "ang"), ("trig", "slnr"), "rS_"])
    kb.op("V", lambda e, o=rC_, i=rpR, j=cosR: e.tensor_tensor(o, i, j, ALU.mult),
          reads=[kR("rp_p"), kR("cos")], writes=[("trig", "rp_m"), "rC_"])
    t1_ = _trig_scratch[0]; t2_ = _trig_scratch[1]
    kb.op("V", lambda e: e.engine_nop(), reads=["rS_", "rC_"],
          writes=[("trig", "cos"), ("trig", "sin"), "t1scr"])
    tt(t1_, rS_, p11_bc, ALU.mult, ["rS_", "p11_bc", "t1scr"], ["t1_"])
    tt(t1_, t1_, rC_, ALU.add, ["t1_", "rC_"], ["t1_"])
    tt(EL1, t1_, c1_bc, ALU.mult, ["t1_", "c1_bc"], ["EL1"])
    tt(t2_, rS_, p12_bc, ALU.mult, ["rS_", "p12_bc", "t1scr"], ["t2_"])
    tt(t2_, t2_, c2_bc, ALU.mult, ["t2_", "c2_bc"], ["t2_"])
    tt(EL1, EL1, t2_, ALU.add, ["EL1", "t2_"], ["EL1"])
    tt(t1_, rS_, p21_bc, ALU.mult, ["rS_", "p21_bc"], ["t1_"])
    tt(EL2, t1_, c1_bc, ALU.mult, ["t1_", "c1_bc"], ["EL2"])
    tt(t2_, rS_, p22_bc, ALU.mult, ["rS_", "p22_bc"], ["t2_"])
    tt(t2_, t2_, rC_, ALU.add, ["t2_", "rC_"], ["t2_"])
    tt(t2_, t2_, c2_bc, ALU.mult, ["t2_", "c2_bc"], ["t2_"])
    tt(EL2, EL2, t2_, ALU.add, ["EL2", "t2_"], ["EL2"])
    a128 = ctx.l1a128
    ang128 = _trig_scratch[4][0:1, :]
    r128 = _trig_scratch[2][0:1, :]
    c128 = _trig_scratch[0][0:1, :]
    s128 = _trig_scratch[1][0:1, :]
    rs128 = _trig_scratch[3][0:1, :]
    rc128 = ctx.carve("f32", P2)[0:1, :]
    kb.op("V", lambda e: e.engine_nop(),
          reads=["t1_", "t2_", "rC_", "rS_", ("trig", "rp_p"), "EL1", "EL2"],
          writes=["a128scr"])
    lnSD = lnrD
    ts(ang128, thD, 128.0, None, ALU.mult, ALU.bypass, ["thD", "a128scr"], ["ang128"])
    i128 = ctx.trig_i32[0:1, :]
    w128 = _trig_scratch_w[0][0:1, :]
    def _red128(dst, phase, dk):
        kb.op("V", lambda e, o=dst, i=ang128, p=phase:
              e.tensor_scalar(o, i, p, None, ALU.add) if p != 0.0 else e.tensor_copy(o, i),
              reads=["ang128", "a128scr", ("trig", "itr"), ("trig", "trw")], writes=[dk])
        kb.op("V", lambda e, o=i128, i=dst: e.tensor_scalar(o, i, 1.0 / (2 * PI), None, ALU.mult),
              reads=[dk], writes=["i128"])
        kb.op("V", lambda e, o=dst, i=i128, a=dst:
              e.scalar_tensor_tensor(o, i, -2 * PI, a, ALU.mult, ALU.add),
              reads=[dk, "i128"], writes=[dk])
        kb.op("V", lambda e, o=w128, i=dst: e.tensor_scalar(o, i, 0.0, 2 * PI, ALU.is_lt, ALU.mult),
              reads=[dk], writes=["w128"])
        kb.op("V", lambda e, o=dst, i=dst, a=w128: e.tensor_tensor(o, i, a, ALU.add),
              reads=[dk, "w128"], writes=[dk])
        act(dst, dst, AF.Sin, [dk], [dk], bias=-PI)
        kb.op("V", lambda e, o=dst, i=dst: e.tensor_scalar(o, i, -1.0, None, ALU.mult),
              reads=[dk], writes=[dk])
    _red128(c128, 0.5 * PI, "c128")
    _red128(s128, 0.0, "s128")
    ts(r128, lnSD, -64.0, None, ALU.mult, ALU.bypass, ["lnrD", "a128scr"], ["r128a"])
    act(r128, r128, AF.Exp, ["r128a"], ["r128"])
    kb.op("V", lambda e, o=rs128, i=r128, j=s128: e.tensor_tensor(o, i, j, ALU.mult),
          reads=["r128", "s128", "a128scr"], writes=["rs128"])
    tt(rc128, r128, c128, ALU.mult, ["r128", "c128"], ["rc128"])
    tt(a128[0], rs128, p11D, ALU.mult, ["rs128", "p11D"], ["a11r"])
    tt(a128[0], a128[0], rc128, ALU.add, ["a11r", "rc128"], ["a11r"])
    tt(a128[1], rs128, p12D, ALU.mult, ["rs128", "p12D"], ["a12r"])
    tt(a128[2], rs128, p21D, ALU.mult, ["rs128", "p21D"], ["a21r"])
    tt(a128[3], rs128, p22D, ALU.mult, ["rs128", "p22D"], ["a22r"])
    tt(a128[3], a128[3], rc128, ALU.add, ["a22r", "rc128"], ["a22r"])
    ctx.l1rows = dict(p21D=p21D, p22D=p22D, a128=a128)
    ctx.l1tabs = dict(TU1=TU1, TU2=TU2, TV1=TV1, TV2=TV2, CU1=CU1, CU2=CU2,
                      CV1=CV1, CV2=CV2, A21c=A21c, A22c=A22c, EL1=EL1, EL2=EL2)

    kb.phase_flush()
    ctx.reset("L1T", "f32"); ctx.reset("L1T", "bf")
    Bp = [ctx.carve("bf", P2) for _ in range(8)]
    bst = ctx.carve("f32", P2)
    for kt in range(8):
        kb.dma("Q", bst, io["B_pre"][kt * 128:(kt + 1) * 128, :], writes=["bst"])
        kb.op("V", lambda e, o=Bp[kt], i=bst: e.tensor_copy(o, i),
              reads=["bst"], writes=[("Bp", kt)])
    x1t = ctx.carve("f32", 1024)
    xn1 = ctx.carve("f32", 1024)
    xn1b = ctx.carve("bf", 1024)
    xn1T = [ctx.carve("bf", 128) for _ in range(8)]
    IN = ctx.carve("f32", 2048)
    locrow = ctx.carve("f32", 1024)[0:1, :]
    xmtmp = ctx.carve("f32", 1)
    locsb = ctx.carve("f32", P2)
    ctx.dbg_locsb = locsb[0:32, :]
    kb.op("V", lambda e, o=locsb: e.memset(o, 0.0), writes=[("loc1", -1)])
    locsb2 = ctx.carve("f32", P2)
    kb.op("V", lambda e, o=locsb2: e.memset(o, 0.0), writes=[("loc2", -1)])
    elbuf = ctx.carve("f32", P2)
    elacc = ctx.carve("f32", P2)
    ones_col = ctx.carve("f32", 1)
    kb.op("V", lambda e, o=ones_col: e.memset(o, 1.0), writes=["ones_col"])
    PS_bu, PS_o1, PS_o2, PS_o3, PS_o4 = (ctx.ps_f[i] for i in range(5))
    TU1, TU2, TV1, TV2 = (ctx.l1tabs[k] for k in ("TU1", "TU2", "TV1", "TV2"))
    CU1, CU2, CV1, CV2 = (ctx.l1tabs[k] for k in ("CU1", "CU2", "CV1", "CV2"))
    EL1, EL2 = ctx.l1tabs["EL1"], ctx.l1tabs["EL2"]

    for J in range(nch):
        kb.dma("Q", x1t, io["x1"][J * 128:(J + 1) * 128, :], writes=["x1t"],
               reads=[("x1_done", J)])
        kb.op("V", lambda e, o=xn1, i=x1t, m=mean1[:, J:J + 1], r=rstd1[:, J:J + 1]:
              e.tensor_scalar(o, i, m, None, ALU.subtract),
              reads=["x1t", "mean1", "rstd1"], writes=["xn1"])
        kb.op("V", lambda e, o=xn1, r=rstd1[:, J:J + 1]:
              e.tensor_scalar(o, o, r, None, ALU.mult),
              reads=["xn1"], writes=["xn1"])
        kb.op("V", lambda e, o=xmtmp, i=xn1: e.reduce_max(o, i, axis=AX.X,
              apply_absolute_value=True), reads=["xn1"], writes=["xmc"])
        if J == 0:
            kb.op("V", lambda e, o=xmaxc, i=xmtmp: e.tensor_copy(o, i),
                  reads=["xmc"], writes=["xmax"])
        else:
            kb.op("V", lambda e, o=xmaxc, i=xmaxc, c=xmtmp: e.tensor_tensor(o, i, c, ALU.max),
                  reads=["xmc", "xmax"], writes=["xmax"])
        kb.op("S", lambda e, o=xn1b, i=xn1: e.copy(o, i), reads=["xn1"], writes=["xn1b"])
        for kt in range(8):
            transpose128(ctx, xn1b[:, kt * 128:(kt + 1) * 128], xn1T[kt],
                         ["xn1b"], ("xn1T", kt))
        for kt in range(8):
            kb.mm(lambda e, o=PS_bu.ap(), l=xn1T[kt], r=Bp[kt], st=(kt == 0), sp=(kt == 7):
                  e.matmul(o, l, r, start=st, stop=sp),
                  reads=[("xn1T", kt), ("Bp", kt)], writes=["psf0"],
                  meta=("psf0", kt == 0, kt == 7))
        for tbl, tk, c0 in ((TU1, "TU1", 0), (TU2, "TU2", 512), (TV1, "TV1", 1024), (TV2, "TV2", 1536)):
            kb.op("V", lambda e, o=IN[:, c0:c0 + 512], t=tbl, b=PS_bu.ap():
                  e.tensor_tensor(o, t, b, ALU.mult),
                  reads=["psf0", tk], writes=["IN"])
        kb.op("V", lambda e, o=elbuf, t=EL1, b=PS_bu.ap(): e.tensor_tensor(o, t, b, ALU.mult),
              reads=["psf0", "EL1"], writes=["elbuf"])
        kb.mm(lambda e, o=ctx.ps_S.ap()[0:1, 0:512], l=ones_col, r=elbuf:
              e.matmul(o, l, r, start=True, stop=True),
              reads=["elbuf", "ones_col"], writes=["psrow"], meta=(("psS", 0), True, True))
        kb.op("V", lambda e, o=elbuf, t=EL2, b=PS_bu.ap(): e.tensor_tensor(o, t, b, ALU.mult),
              reads=["psf0", "EL2", "psrow"], writes=["elbuf"])
        kb.mm(lambda e, o=ctx.ps_S.ap()[0:1, 512:1024], l=ones_col, r=elbuf:
              e.matmul(o, l, r, start=True, stop=True),
              reads=["elbuf", "ones_col"], writes=["psrow2"], meta=(("psS", 1), True, True))
        kb.op("V", lambda e, o=locrow, i=ctx.ps_S.ap()[0:1, 0:1024]: e.tensor_copy(o, i),
              reads=["psrow", "psrow2"], writes=["locrow"])
        kb.dma("P", locsb[J:J + 1, :], locrow[:, 0:512], reads=["locrow"], writes=[("loc1", J)])
        kb.dma("P", locsb2[J:J + 1, :], locrow[:, 512:1024], reads=["locrow"], writes=[("loc2", J)])
        for b, psn in ((0, PS_o1), (1, PS_o2), (2, PS_o3), (3, PS_o4)):
            kb.mm(lambda e, o=psn.ap(), l=ctx.triu_f, r=IN[:, b * 512:(b + 1) * 512]:
                  e.matmul(o, l, r, start=True, stop=True),
                  reads=["IN", "c_triu"], writes=["psf%d" % (b + 1)],
                  meta=("psf%d" % (b + 1), True, True))
        yj = ys[J]
        kb.op("V", lambda e, o=elacc, t=CU1, b=PS_o1.ap(): e.tensor_tensor(o, t, b, ALU.mult),
              reads=["psf1", "CU1"], writes=["elacc"])
        for tbl, tk, psn, pk in ((CU2, "CU2", PS_o2, "psf2"), (CV1, "CV1", PS_o3, "psf3"),
                                 (CV2, "CV2", PS_o4, "psf4")):
            kb.op("V", lambda e, o=elbuf, t=tbl, b=psn.ap(): e.tensor_tensor(o, t, b, ALU.mult),
                  reads=[pk, tk], writes=["elbuf"])
            kb.op("V", lambda e, o=elacc, i=elacc, a=elbuf: e.tensor_tensor(o, i, a, ALU.add),
                  reads=["elbuf", "elacc"], writes=["elacc"])
        kb.op("S", lambda e, o=yj, i=elacc: e.copy(o, i), reads=["elacc"], writes=[("ys", J)])

    l1c = ctx.carve("f32", 128)
    l2c = ctx.carve("f32", 128)
    locdeps1 = [("loc1", Jx) for Jx in range(-1, nch)]
    locdeps2 = [("loc2", Jx) for Jx in range(-1, nch)]
    for t in range(4):
        kb.mm(lambda e, o=ctx.ps_f[0].ap()[:, 0:32], i=locsb[0:32, t * 128:(t + 1) * 128],
              idn=ctx.ident_f32[0:32, 0:32]: e.transpose(o, i, idn),
              reads=locdeps1 + ["c_identf"], writes=["psf0"], meta=("psf0", True, True))
        kb.op("V", lambda e, o=l1c[:, t * 32:(t + 1) * 32], i=ctx.ps_f[0].ap()[:, 0:32]:
              e.tensor_copy(o, i), reads=["psf0"], writes=["l1c"])
        kb.mm(lambda e, o=ctx.ps_f[1].ap()[:, 0:32], i=locsb2[0:32, t * 128:(t + 1) * 128],
              idn=ctx.ident_f32[0:32, 0:32]: e.transpose(o, i, idn),
              reads=locdeps2 + ["c_identf"], writes=["psf1"], meta=("psf1", True, True))
        kb.op("V", lambda e, o=l2c[:, t * 32:(t + 1) * 32], i=ctx.ps_f[1].ap()[:, 0:32]:
              e.tensor_copy(o, i), reads=["psf1"], writes=["l2c"])
    arows = ctx.carve("f32", P2)
    for i, rkey in enumerate(("a11r", "a12r", "a21r", "a22r")):
        kb.dma("P", arows[i:i + 1, :], ctx.l1rows["a128"][i], reads=[rkey],
               writes=[("arows", i)])
    acols = ctx.carve("f32", 16)
    for t in range(4):
        kb.mm(lambda e, o=ctx.ps_f[2].ap()[:, 0:4], i=arows[0:4, t * 128:(t + 1) * 128],
              idn=ctx.ident_f32[0:4, 0:4]: e.transpose(o, i, idn),
              reads=[("arows", i) for i in range(4)] + ["c_identf"], writes=["psf2"],
              meta=("psf2", True, True))
        kb.op("V", lambda e, o=acols[:, t * 4:(t + 1) * 4], i=ctx.ps_f[2].ap()[:, 0:4]:
              e.tensor_copy(o, i), reads=["psf2"], writes=["acols"])
    av = acols.rearrange("p (t e) -> p t e", e=4)
    aE = lambda e_: av[:, :, e_:e_ + 1]
    l1v = l1c.rearrange("p (t j) -> p t j", j=32)
    l2v = l2c.rearrange("p (t j) -> p t j", j=32)
    c1c = ctx.carve("f32", 4)[:, :, None]
    c2c = ctx.carve("f32", 4)[:, :, None]
    csn1 = ctx.carve("f32", 128)
    csn2 = ctx.carve("f32", 128)
    ctx.dbg_csn1 = csn1; ctx.dbg_csn2 = csn2
    kb.op("V", lambda e, o=csn1: e.memset(o, 0.0), writes=["csn1"])
    kb.op("V", lambda e, o=csn2: e.memset(o, 0.0), writes=["csn2"])
    tA = ctx.carve("f32", 4)[:, :, None]
    tB = ctx.carve("f32", 4)[:, :, None]
    tC = ctx.carve("f32", 4)[:, :, None]
    kb.op("V", lambda e, o=c1c: e.memset(o, 0.0), writes=["c1c"])
    kb.op("V", lambda e, o=c2c: e.memset(o, 0.0), writes=["c2c"])
    for J in range(nch):
        kb.op("V", lambda e, o=csn1[:, 4 * J:4 * J + 4][:, :, None], i=c1c: e.tensor_copy(o, i),
              reads=["c1c"], writes=["csn1"])
        kb.op("V", lambda e, o=csn2[:, 4 * J:4 * J + 4][:, :, None], i=c2c: e.tensor_copy(o, i),
              reads=["c2c"], writes=["csn2"])
        kb.op("V", lambda e, o=tA, i=aE(0), c=c1c: e.tensor_tensor(o, i, c, ALU.mult),
              reads=["acols", "c1c"], writes=["tA"])
        kb.op("V", lambda e, o=tB, i=aE(1), c=c2c: e.tensor_tensor(o, i, c, ALU.mult),
              reads=["acols", "c2c"], writes=["tB"])
        kb.op("V", lambda e, o=tA, i=tA, c=tB: e.tensor_tensor(o, i, c, ALU.add),
              reads=["tA", "tB"], writes=["tA"])
        kb.op("V", lambda e, o=tA, i=tA, c=l1v[:, :, J:J + 1]: e.tensor_tensor(o, i, c, ALU.add),
              reads=["tA", "l1c"], writes=["tA"])
        kb.op("V", lambda e, o=tB, i=aE(2), c=c1c: e.tensor_tensor(o, i, c, ALU.mult),
              reads=["acols", "c1c"], writes=["tB"])
        kb.op("V", lambda e, o=tC, i=aE(3), c=c2c: e.tensor_tensor(o, i, c, ALU.mult),
              reads=["acols", "c2c"], writes=["tC"])
        kb.op("V", lambda e, o=tB, i=tB, c=tC: e.tensor_tensor(o, i, c, ALU.add),
              reads=["tB", "tC"], writes=["tB"])
        kb.op("V", lambda e, o=c2c, i=tB, c=l2v[:, :, J:J + 1]: e.tensor_tensor(o, i, c, ALU.add),
              reads=["tB", "l2c"], writes=["c2c"])
        kb.op("V", lambda e, o=c1c, i=tA: e.tensor_copy(o, i), reads=["tA"], writes=["c1c"])
    ctm1 = ctx.carve("f32", 128)
    ctm2 = ctx.carve("f32", 128)
    kb.op("V", lambda e, o=ctm1.rearrange("p (t j) -> p t j", j=32),
          i=csn1.rearrange("p (j t) -> p t j", t=4): e.tensor_copy(o, i),
          reads=["csn1"], writes=["ctm1"])
    kb.op("V", lambda e, o=ctm2.rearrange("p (t j) -> p t j", j=32),
          i=csn2.rearrange("p (j t) -> p t j", t=4): e.tensor_copy(o, i),
          reads=["csn2"], writes=["ctm2"])
    crows1 = ctx.carve("f32", P2)
    crows2 = ctx.carve("f32", P2)
    ctx.dbg_crows1 = crows1; ctx.dbg_crows2 = crows2
    for t in range(4):
        kb.mm(lambda e, o=ctx.ps_f[3].ap()[0:32, 0:128], i=ctm1[:, t * 32:(t + 1) * 32],
              idn=ctx.ident_f32: e.transpose(o, i, idn),
              reads=["ctm1", "c_identf"], writes=["psf3"], meta=("psf3", True, True))
        kb.op("V", lambda e, o=crows1[0:32, t * 128:(t + 1) * 128], i=ctx.ps_f[3].ap()[0:32, 0:128]:
              e.tensor_copy(o, i), reads=["psf3"], writes=["crows1"])
        kb.mm(lambda e, o=ctx.ps_f[4].ap()[0:32, 0:128], i=ctm2[:, t * 32:(t + 1) * 32],
              idn=ctx.ident_f32: e.transpose(o, i, idn),
              reads=["ctm2", "c_identf"], writes=["psf4"], meta=("psf4", True, True))
        kb.op("V", lambda e, o=crows2[0:32, t * 128:(t + 1) * 128], i=ctx.ps_f[4].ap()[0:32, 0:128]:
              e.tensor_copy(o, i), reads=["psf4"], writes=["crows2"])
    Ball = ctx.carve("f32", 4096)[0:32, :]
    kb.dma("Q", Ball, io["Ball"], writes=["Ball"])
    A21c, A22c = ctx.l1tabs["A21c"], ctx.l1tabs["A22c"]
    for J in range(1, nch):
        kb.mm(lambda e, o=ctx.ps_f[0].ap(), l=Ball[:, J * 128:(J + 1) * 128], r=crows1[0:32, :],
              st=True, sp=True: e.matmul(o, l, r, start=st, stop=sp),
              reads=["Ball", "crows1"], writes=["psf0"], meta=("psf0", True, True))
        kb.mm(lambda e, o=ctx.ps_f[1].ap(), l=Ball[:, J * 128:(J + 1) * 128], r=crows2[0:32, :],
              st=True, sp=True: e.matmul(o, l, r, start=st, stop=sp),
              reads=["Ball", "crows2"], writes=["psf1"], meta=("psf1", True, True))
        kb.op("V", lambda e, o=elbuf, t=A21c, b=ctx.ps_f[0].ap(): e.tensor_tensor(o, t, b, ALU.mult),
              reads=["psf0", "A21c"], writes=["elbuf"])
        kb.op("V", lambda e, o=ys[J], i=ys[J], a=elbuf: e.tensor_tensor(o, i, a, ALU.add),
              reads=["elbuf", ("ys", J)], writes=[("ys", J)])
        kb.op("V", lambda e, o=elbuf, t=A22c, b=ctx.ps_f[1].ap(): e.tensor_tensor(o, t, b, ALU.mult),
              reads=["psf1", "A22c"], writes=["elbuf"])
        kb.op("V", lambda e, o=ys[J], i=ys[J], a=elbuf: e.tensor_tensor(o, i, a, ALU.add),
              reads=["elbuf", ("ys", J)], writes=[("ys", J)])


def build_l1_cy(ctx, io, L):
    nc, kb = ctx.nc, ctx.kb
    kb.phase_flush()
    ctx.reset("L1S", "bf"); ctx.reset("L1S", "f32")
    nch = ctx.l1nch
    ys = ctx.l1ys
    mean1, rstd1 = ctx.l1["mean1"], ctx.l1["rstd1"]
    ysT = [ctx.carve("bf", L) for _ in range(4)]
    for J in range(nch):
        for t in range(4):
            transpose128(ctx, ys[J][:, t * 128:(t + 1) * 128],
                         ysT[t][:, J * 128:(J + 1) * 128], [("ys", J)], ("ysT", t))
    cst_ = ctx.carve("f32", 1024)
    altc = ctx.carve("f32", 1)
    kb.dma("Q", altc, io["altcol"], writes=["altc"])
    CT = [ctx.carve("bf", 1024) for _ in range(4)]
    for t in range(4):
        kb.dma("Q", cst_, io["C_pre"][t * 128:(t + 1) * 128, :], writes=["cst_"])
        kb.op("V", lambda e, o=CT[t], i=cst_, s=altc: e.tensor_scalar(o, i, s, None, ALU.mult),
              reads=["cst_", "altc"], writes=[("CT", t)])
    drow = ctx.carve("f32", 1024)[0:1, :]
    D_bc = ctx.carve("f32", 1024)
    kb.dma("Q", drow, io["D_vec"][None, :], writes=["drow"])
    bcast_row_f32(ctx, drow, D_bc, ["drow"], "D_bc", 2)
    x1t2 = ctx.carve("f32", 1024)
    xn1t2 = ctx.carve("f32", 1024)
    at1 = ctx.carve("f32", 1024)
    for mt in range(L // 128):
        for half in range(2):
            pp = ctx.ps_f[half]
            pk = "psf%d" % half
            for t in range(4):
                kb.mm(lambda e, o=pp.ap(), l=ysT[t][:, mt * 128:(mt + 1) * 128],
                      r=CT[t][:, half * 512:(half + 1) * 512], st=(t == 0), sp=(t == 3):
                      e.matmul(o, l, r, start=st, stop=sp),
                      reads=[("ysT", t), ("CT", t)], writes=[pk], meta=(pk, t == 0, t == 3))
        kb.dma("Q", x1t2, io["x1"][mt * 128:(mt + 1) * 128, :], writes=["x1t2"],
               reads=[("x1_done", mt)])
        kb.op("V", lambda e, o=xn1t2, i=x1t2, m=mean1[:, mt:mt + 1], r=rstd1[:, mt:mt + 1]:
              e.tensor_scalar(o, i, m, None, ALU.subtract),
              reads=["x1t2", "mean1", "rstd1"], writes=["xn1t2"])
        kb.op("V", lambda e, o=xn1t2, r=rstd1[:, mt:mt + 1]:
              e.tensor_scalar(o, o, r, None, ALU.mult),
              reads=["xn1t2"], writes=["xn1t2"])
        kb.op("V", lambda e, o=at1, i=xn1t2, d=D_bc: e.tensor_tensor(o, i, d, ALU.mult),
              reads=["xn1t2", "D_bc"], writes=["at1"])
        for half in range(2):
            pk = "psf%d" % half
            kb.op("V", lambda e, o=at1[:, half * 512:(half + 1) * 512],
                  i=at1[:, half * 512:(half + 1) * 512], b=ctx.ps_f[half].ap():
                  e.tensor_tensor(o, i, b, ALU.add), reads=[pk, "at1"], writes=["at1"])
        kb.dma("P", io["attn1"][mt * 128:(mt + 1) * 128, :], at1,
               reads=["at1"], writes=[("attn1_done", mt)])


def _colmax_to_scalar(ctx, col_ap, colkey, out_row, outkey, tmpkey):
    kb = ctx.kb
    kb.mm(lambda e, o=ctx.ps_f[4].ap()[0:1, 0:128], i=col_ap, idn=ctx.ident_f32:
          e.transpose(o, i, idn), reads=[colkey, "c_identf"], writes=["psf4"],
          meta=("psf4", True, True))
    kb.op("V", lambda e, o=out_row, i=ctx.ps_f[4].ap()[0:1, 0:128]:
          e.reduce_max(o, i, axis=AX.X), reads=["psf4"], writes=[outkey])


def emit_round_int(ctx, out_ap, in_ap, rs_col, int_tmp, tmp2, rkeys, wkey, rskey, tag):
    kb = ctx.kb
    k = lambda n: (tag, n)
    kb.op("V", lambda e, o=int_tmp, i=in_ap, s=rs_col: e.tensor_scalar(o, i, s, None, ALU.mult),
          reads=list(rkeys) + [rskey], writes=[k("t1")])
    kb.op("V", lambda e, o=out_ap, i=int_tmp: e.tensor_copy(o, i),
          reads=[k("t1")], writes=[wkey])


def build_l1_ffn1(ctx, io, L, NC):
    nc, kb = ctx.nc, ctx.kb
    kb.phase_flush()
    ctx.reset_if("L1Y", "bf"); ctx.reset_if("L1Y", "f32")
    nmt = L // 128
    tt, ts, act = _row_ops(ctx)
    r1 = lambda w=1: ctx.carve("f32", w)[0:1, :]
    if getattr(ctx, "l1", None) is None:
        emit_l1_stats(ctx, io, L)
    mean1, rstd1 = ctx.l1["mean1"], ctx.l1["rstd1"]
    gmax = r1(NC)
    kb.dma("Q", gmax, io["xmaxs"], reads=io.get("xmaxs_dep", []), writes=["gmax_in"])
    sxr, rsxr, swr, rswr, ssr = r1(), r1(), r1(), r1(), r1()
    kb.op("V", lambda e, o=sxr, i=gmax: e.reduce_max(o, i, axis=AX.X),
          reads=["gmax_in"], writes=["sxr"])
    ts(sxr, sxr, 1e-6, 1.0 / 127.0, ALU.max, ALU.mult, ["sxr"], ["sxr"])
    kb.op("V", lambda e, o=rsxr, i=sxr: e.reciprocal(o, i), reads=["sxr"], writes=["rsxr"])
    wst = ctx.carve("f32", 1024)
    wmcols = ctx.carve("f32", 32)
    for rt in range(32):
        kb.dma("Q", wst, io["w1"][rt * 128:(rt + 1) * 128, :], writes=["wst1"])
        kb.op("V", lambda e, o=wmcols[:, rt:rt + 1], i=wst: e.reduce_max(o, i, axis=AX.X,
              apply_absolute_value=True), reads=["wst1"], writes=[("wmc", rt)])
    wmax = ctx.carve("f32", 1)
    kb.op("V", lambda e, o=wmax, i=wmcols: e.reduce_max(o, i, axis=AX.X),
          reads=[("wmc", rt) for rt in range(32)], writes=["wmax1"])
    _colmax_to_scalar(ctx, wmax, "wmax1", swr, "swr", None)
    ts(swr, swr, 1e-6, 1.0 / 127.0, ALU.max, ALU.mult, ["swr"], ["swr"])
    kb.op("V", lambda e, o=rswr, i=swr: e.reciprocal(o, i), reads=["swr"], writes=["rswr"])
    tt(ssr, swr, sxr, ALU.mult, ["swr", "sxr"], ["ssr"])
    rsw_col = ctx.carve("f32", 1)
    rsx_col = ctx.carve("f32", 1)
    ss_col = ctx.carve("f32", 1)
    for (row, rk, col, ck) in ((rswr, "rswr", rsw_col, "rsw_col"),
                               (rsxr, "rsxr", rsx_col, "rsx_col"),
                               (ssr, "ssr", ss_col, "ss_col")):
        kb.mm(lambda e, o=ctx.ps_f[4].ap()[:, 0:1], l=ctx.ones_f, r=row:
              e.matmul(o, l, r, start=True, stop=True),
              reads=[rk, "c_ones"], writes=["psf4"], meta=("psf4", True, True))
        kb.op("V", lambda e, o=col, i=ctx.ps_f[4].ap()[:, 0:1]: e.tensor_copy(o, i),
              reads=["psf4"], writes=[ck])
    i1k = ctx.carve("i32", 1024)
    wqb = ctx.carve("bf", 1024)
    trs2 = ctx.carve("bf", 128)
    for rt in range(32):
        kb.dma("Q", wst, io["w1"][rt * 128:(rt + 1) * 128, :], writes=["wst1"])
        emit_round_int(ctx, wqb, wst, rsw_col, i1k, None, ["wst1"], "wqb", "rsw_col", "rq")
        for ct in range(8):
            transpose128(ctx, wqb[:, ct * 128:(ct + 1) * 128], trs2, ["wqb"], "trs2")
            kb.dma("P", io["w1qT_s"][ct * 128:(ct + 1) * 128, rt * 128:(rt + 1) * 128],
                   trs2, reads=["trs2"], writes=[("w1T", rt, ct)])
    kb.barrier_region("w1T_all", "V", [("w1T", rt, ct) for rt in range(32) for ct in range(8)])
    b1row = r1(1024)
    b1s = ctx.carve("f32", 4096)
    kb.dma("Q", b1row, io["b1"][None, 0:1024], writes=["b1r0"])
    b2row = r1(1024)
    kb.dma("Q", b2row, io["b1"][None, 1024:2048], writes=["b1r1"])
    b3row = r1(1024)
    kb.dma("Q", b3row, io["b1"][None, 2048:3072], writes=["b1r2"])
    b4row = r1(1024)
    kb.dma("Q", b4row, io["b1"][None, 3072:4096], writes=["b1r3"])
    for i, (row, rk) in enumerate(((b1row, "b1r0"), (b2row, "b1r1"), (b3row, "b1r2"), (b4row, "b1r3"))):
        bcast_row_f32(ctx, row, b1s[:, i * 1024:(i + 1) * 1024], [rk], "b1s", 3)
    kb.op("V", lambda e, o=b1s, i=b1s, s=ss_col: e.tensor_scalar(o, i, s, None, ALU.mult),
          reads=["b1s", "ss_col"], writes=["b1s"])
    x1t = ctx.carve("f32", 1024)
    xn1 = ctx.carve("f32", 1024)
    xqb = ctx.carve("bf", 1024)
    xqT = [ctx.carve("bf", L) for _ in range(8)]
    for mt in range(nmt):
        kb.dma("Q", x1t, io["x1"][mt * 128:(mt + 1) * 128, :], writes=["x1t"],
               reads=[("x1_done", mt)])
        kb.op("V", lambda e, o=xn1, i=x1t, m=mean1[:, mt:mt + 1], r=rstd1[:, mt:mt + 1]:
              e.tensor_scalar(o, i, m, None, ALU.subtract),
              reads=["x1t", "mean1", "rstd1"], writes=["xn1f"])
        kb.op("V", lambda e, o=xn1, r=rstd1[:, mt:mt + 1]:
              e.tensor_scalar(o, o, r, None, ALU.mult),
              reads=["xn1f"], writes=["xn1f"])
        emit_round_int(ctx, xqb, xn1, rsx_col, i1k, None, ["xn1f"], "xqb", "rsx_col", "rq")
        for kt in range(8):
            transpose128(ctx, xqb[:, kt * 128:(kt + 1) * 128],
                         xqT[kt][:, mt * 128:(mt + 1) * 128], ["xqb"], ("xqT", kt))
    pan1 = [ctx.carve("bf", 512) for _ in range(8)]
    hvt = ctx.carve("f32", 512)
    hmc = ctx.carve("f32", 32 * 8)
    kb.op("V", lambda e, o=hmc: e.memset(o, 0.0), writes=[("hmc", -1, -1)])
    for ng in range(8):
        for kt in range(8):
            kb.dma("Q", pan1[kt], io["w1qT_s"][kt * 128:(kt + 1) * 128, ng * 512:(ng + 1) * 512],
                   reads=["w1T_all"], writes=[("pan1", kt)])
        for mt in range(nmt):
            psn = mt % 2
            pk = "psf%d" % psn
            for kt in range(8):
                kb.mm(lambda e, o=ctx.ps_f[psn].ap(), l=xqT[kt][:, mt * 128:(mt + 1) * 128],
                      r=pan1[kt], st=(kt == 0), sp=(kt == 7): e.matmul(o, l, r, start=st, stop=sp),
                      reads=[("xqT", kt), ("pan1", kt)], writes=[pk], meta=(pk, kt == 0, kt == 7))
            kb.op("V", lambda e, o=hvt, i=ctx.ps_f[psn].ap(), s=ss_col, b=b1s[:, ng * 512:(ng + 1) * 512]:
                  e.scalar_tensor_tensor(o, i, s, b, ALU.mult, ALU.add),
                  reads=[pk, "ss_col", "b1s"], writes=["hvt"])
            kb.op("S", lambda e, o=hvt, i=hvt: e.activation(o, i, AF.Gelu),
                  reads=["hvt"], writes=["hvt"])
            kb.op("V", lambda e, o=hmc[:, ng * 32 + mt:ng * 32 + mt + 1], i=hvt:
                  e.reduce_max(o, i, axis=AX.X, apply_absolute_value=True),
                  reads=["hvt"], writes=[("hmc", ng, mt)])
            kb.dma("P", io["h1_s"][mt * 128:(mt + 1) * 128, ng * 512:(ng + 1) * 512], hvt,
                   reads=["hvt"], writes=[("h1d", ng, mt)])
    hmax = ctx.carve("f32", 1)
    kb.op("V", lambda e, o=hmax, i=hmc: e.reduce_max(o, i, axis=AX.X),
          reads=[("hmc", -1, -1)] + [("hmc", ng, mt) for ng in range(8) for mt in range(nmt)],
          writes=["hmax1"])
    hmrow = r1(128)
    _colmax_to_scalar(ctx, hmax, "hmax1", hmrow[:, 0:1], "hmrow", None)
    kb.dma("P", io["hmax_out"], hmrow[:, 0:1], reads=["hmrow"], writes=["hmax_od"])


def build_l1_ffn2(ctx, io, L, NC):
    nc, kb = ctx.nc, ctx.kb
    kb.phase_flush()
    ctx.reset_if("L1Y", "bf"); ctx.reset_if("L1Y", "f32")
    nmt = L // 128
    tt, ts, act = _row_ops(ctx)
    r1 = lambda w=1: ctx.carve("f32", w)[0:1, :]
    gmax = r1(NC)
    kb.dma("Q", gmax, io["hmaxs"], reads=io.get("hmaxs_dep", []), writes=["gh_in"])
    shr, rshr, swr, rswr, ssr = r1(), r1(), r1(), r1(), r1()
    kb.op("V", lambda e, o=shr, i=gmax: e.reduce_max(o, i, axis=AX.X),
          reads=["gh_in"], writes=["shr"])
    ts(shr, shr, 1e-6, 1.0 / 127.0, ALU.max, ALU.mult, ["shr"], ["shr"])
    kb.op("V", lambda e, o=rshr, i=shr: e.reciprocal(o, i), reads=["shr"], writes=["rshr"])
    wst = ctx.carve("f32", 4096)
    wmcols = ctx.carve("f32", 8)
    for rt in range(8):
        kb.dma("Q", wst, io["w2"][rt * 128:(rt + 1) * 128, :], writes=["wst2"])
        kb.op("V", lambda e, o=wmcols[:, rt:rt + 1], i=wst: e.reduce_max(o, i, axis=AX.X,
              apply_absolute_value=True), reads=["wst2"], writes=[("wmc2", rt)])
    wmax = ctx.carve("f32", 1)
    kb.op("V", lambda e, o=wmax, i=wmcols[:, 0:8]: e.reduce_max(o, i, axis=AX.X),
          reads=[("wmc2", rt) for rt in range(8)], writes=["wmax2"])
    _colmax_to_scalar(ctx, wmax, "wmax2", swr, "swr2", None)
    ts(swr, swr, 1e-6, 1.0 / 127.0, ALU.max, ALU.mult, ["swr2"], ["swr2"])
    kb.op("V", lambda e, o=rswr, i=swr: e.reciprocal(o, i), reads=["swr2"], writes=["rswr2"])
    tt(ssr, swr, shr, ALU.mult, ["swr2", "shr"], ["ssr2"])
    rsw_col = ctx.carve("f32", 1)
    rsh_col = ctx.carve("f32", 1)
    ss_col = ctx.carve("f32", 1)
    for (row, rk, col, ck) in ((rswr, "rswr2", rsw_col, "rsw_col2"),
                               (rshr, "rshr", rsh_col, "rsh_col"),
                               (ssr, "ssr2", ss_col, "ss_col2")):
        kb.mm(lambda e, o=ctx.ps_f[4].ap()[:, 0:1], l=ctx.ones_f, r=row:
              e.matmul(o, l, r, start=True, stop=True),
              reads=[rk, "c_ones"], writes=["psf4"], meta=("psf4", True, True))
        kb.op("V", lambda e, o=col, i=ctx.ps_f[4].ap()[:, 0:1]: e.tensor_copy(o, i),
              reads=["psf4"], writes=[ck])
    i4k = ctx.carve("i32", 4096)
    wqb = ctx.carve("bf", 4096)
    w2qT = [ctx.carve("bf", 1024) for _ in range(32)]
    for rt in range(8):
        kb.dma("Q", wst, io["w2"][rt * 128:(rt + 1) * 128, :], writes=["wst2"])
        emit_round_int(ctx, wqb, wst, rsw_col, i4k, None, ["wst2"], "wqb2", "rsw_col2", "rq2")
        for ct in range(32):
            transpose128(ctx, wqb[:, ct * 128:(ct + 1) * 128],
                         w2qT[ct][:, rt * 128:(rt + 1) * 128], ["wqb2"], ("w2qT", ct))
    b2row = r1(1024)
    b2s = ctx.carve("f32", 1024)
    kb.dma("Q", b2row, io["b2"][None, :], writes=["b2r"])
    bcast_row_f32(ctx, b2row, b2s, ["b2r"], "b2s", 3)
    kb.op("V", lambda e, o=b2s, i=b2s, s=ss_col: e.tensor_scalar(o, i, s, None, ALU.mult),
          reads=["b2s", "ss_col2"], writes=["b2s"])
    h1t = ctx.carve("f32", 4096)
    hqb = wqb
    hqT = [ctx.carve("bf", 128) for _ in range(32)]
    x1t = ctx.carve("f32", 1024)
    at1 = ctx.carve("f32", 1024)
    ot = ctx.carve("f32", 1024)
    vt = ctx.carve("f32", 512)
    for mt in range(nmt):
        kb.dma("Q", h1t, io["h1_s"][mt * 128:(mt + 1) * 128, :],
               reads=[("h1d", ng, mt) for ng in range(8)], writes=["h1t"])
        emit_round_int(ctx, hqb, h1t, rsh_col, i4k, None, ["h1t", "wqb2"], "wqb2", "rsh_col", "rq2")
        for kt in range(32):
            transpose128(ctx, hqb[:, kt * 128:(kt + 1) * 128], hqT[kt], ["wqb2"], ("hqT", kt))
        for half in range(2):
            pk = "psf%d" % half
            for kt in range(32):
                kb.mm(lambda e, o=ctx.ps_f[half].ap(), l=hqT[kt],
                      r=w2qT[kt][:, half * 512:(half + 1) * 512], st=(kt == 0), sp=(kt == 31):
                      e.matmul(o, l, r, start=st, stop=sp),
                      reads=[("hqT", kt), ("w2qT", kt)], writes=[pk], meta=(pk, kt == 0, kt == 31))
        kb.dma("Q", x1t, io["x1"][mt * 128:(mt + 1) * 128, :], writes=["x1t2"],
               reads=[("x1_done", mt)])
        kb.dma("Q", at1, io["attn1"][mt * 128:(mt + 1) * 128, :], writes=["at1t"],
               reads=[("attn1_done", mt)])
        kb.op("V", lambda e, o=ot, i=x1t, a=at1: e.tensor_tensor(o, i, a, ALU.add),
              reads=["x1t2", "at1t"], writes=["ot"])
        for half in range(2):
            pk = "psf%d" % half
            kb.op("V", lambda e, o=vt, i=ctx.ps_f[half].ap(), s=ss_col, b=b2s[:, half * 512:(half + 1) * 512]:
                  e.scalar_tensor_tensor(o, i, s, b, ALU.mult, ALU.add),
                  reads=[pk, "ss_col2", "b2s"], writes=["vt"])
            kb.op("V", lambda e, o=ot[:, half * 512:(half + 1) * 512],
                  i=ot[:, half * 512:(half + 1) * 512], v=vt: e.tensor_tensor(o, i, v, ALU.add),
                  reads=["vt", "ot"], writes=["ot"])
        kb.dma("P", io["out"][mt * 128:(mt + 1) * 128, :], ot,
               reads=["ot"], writes=[("out_done", mt)])




NCORES = 8
SEQ = 4096


def _consts():
    c = {}
    c["ident_bf"] = np.eye(128, dtype=np.float32).astype(ml_dtypes.bfloat16)
    c["ident_f"] = np.eye(128, dtype=np.float32)
    c["ones_f"] = np.ones((1, 128), np.float32)
    c["triu_f"] = np.triu(np.ones((128, 128), np.float32))
    c["iota_b"] = np.tile(np.arange(256, dtype=np.float32), (128, 1))
    c["phpi"] = np.concatenate([np.full((64, 1), 0.5 * math.pi),
                                np.zeros((64, 1))]).astype(np.float32)
    c["ddup"] = np.concatenate([np.eye(64), np.eye(64)], axis=1).astype(ml_dtypes.bfloat16)
    c["iota_col"] = np.arange(128, dtype=np.float32)[:, None]
    c["riota_col"] = (127 - np.arange(128, dtype=np.float32))[:, None]
    c["iotap_col"] = (np.arange(128, dtype=np.float32) + 1)[:, None]
    c["Ball"] = np.repeat(np.eye(32, dtype=np.float32), 128, axis=1)
    c["altcol"] = np.tile(np.array([1.0, -1.0], np.float32), 64)[:, None]
    return c


_CONST_SHAPES = {"ident_bf": ([128, 128], BF16), "ident_f": ([128, 128], F32),
                 "ones_f": ([1, 128], F32), "triu_f": ([128, 128], F32),
                 "iota_b": ([128, 256], F32), "phpi": ([128, 1], F32),
                 "ddup": ([64, 128], BF16), "iota_col": ([128, 1], F32),
                 "riota_col": ([128, 1], F32), "iotap_col": ([128, 1], F32),
                 "Ball": ([32, 4096], F32), "altcol": ([128, 1], F32)}


def _emit_consts(ctx, io, names=("ident_bf", "ident_f", "ones_f")):
    kb = ctx.kb
    ctx.ident_bf = ctx.carve("bf", 128)
    kb.dma("Q", ctx.ident_bf, io["ident_bf"], writes=["c_ident"])
    ctx.ident_f32 = ctx.carve("f32", 128)
    kb.dma("Q", ctx.ident_f32, io["ident_f"], writes=["c_identf"])
    ctx.ones_f = ctx.carve("f32", 128)[0:1, :]
    kb.dma("Q", ctx.ones_f, io["ones_f"], writes=["c_ones"])
    if "triu_f" in names:
        ctx.triu_f = ctx.carve("f32", 128)
        kb.dma("Q", ctx.triu_f, io["triu_f"], writes=["c_triu"])


def _mk(nc, io, name, shape, dt=F32, kind="ExternalInput"):
    io[name] = nc.dram_tensor(name, shape, dt, kind=kind).ap()


def build_launch1(L):
    nc = bass.Bass("TRN2", target_bir_lowering=False)
    kb = KB(nc)
    ctx = Ctx(nc, kb, bf_words=40900, f32_words=31872, int_words=768)
    ctx.setup_psum()
    io = {}
    for n, s, d in (("x", [L, 1024], F32), ("wq", [1024, 1024], F32),
                    ("wk", [1024, 1024], F32), ("wv", [1024, 1024], F32),
                    ("wo", [1024, 1024], F32), ("g1", [1024, 64], F32),
                    ("g2", [64, 1024], F32), ("gamma", [1024], F32),
                    ("beta", [1024], F32), ("th_pre", [128, 16], F32),
                    ("wg", [2816, 1024], F32), ("wu", [2816, 1024], F32),
                    ("wd", [1024, 2816], F32), ("steps_raw", [256], F32),
                    ("G_raw", [256], F32), ("A_raw", [256], F32),
                    ("B_pre", [1024, 512], F32), ("C_pre", [512, 1024], F32),
                    ("D_vec", [1024], F32)):
        _mk(nc, io, n, s, d)
    for n, (s, d) in _CONST_SHAPES.items():
        _mk(nc, io, n, s, d)
    for n, s, d in (("ao_s", [L, 1024], F32), ("wgT_s", [1024, 2816], BF16),
                    ("wuT_s", [1024, 2816], BF16), ("wdT_s", [2816, 1024], BF16),
                    ("wo_bf_s", [1024, 1024], BF16), ("wv_bf_s", [1024, 1024], BF16)):
        _mk(nc, io, n, s, d, kind="Internal")
    _mk(nc, io, "x1", [L, 1024], F32, kind="ExternalOutput")
    _mk(nc, io, "attn1", [L, 1024], F32, kind="ExternalOutput")
    _mk(nc, io, "xmax_out", [1, 1], F32, kind="ExternalOutput")
    build_l0(ctx, io, L)
    build_l0b(ctx, io, L)
    build_l1_scan(ctx, io, L)
    build_l1_cy(ctx, io, L)
    build_l1_xmax(ctx, io)
    kb.final_wait([("x1_done", mt) for mt in range(L // 128)]
                  + [("attn1_done", mt) for mt in range(L // 128)] + ["xmax_od"])
    errs = kb.check_psum_groups()
    assert not errs, errs[:4]
    kb.finalize()
    return nc


def build_launch2(L, NC):
    nc = bass.Bass("TRN2", target_bir_lowering=False)
    kb = KB(nc)
    ctx = Ctx(nc, kb, bf_words=40000, f32_words=28000, int_words=1024)
    ctx.setup_psum()
    io = {}
    for n, s, d in (("x1", [L, 1024], F32), ("w1", [4096, 1024], F32),
                    ("b1", [4096], F32), ("xmaxs", [1, NC], F32)):
        _mk(nc, io, n, s, d)
    for n in ("ident_bf", "ident_f", "ones_f"):
        s, d = _CONST_SHAPES[n]
        _mk(nc, io, n, s, d)
    _mk(nc, io, "w1qT_s", [1024, 4096], BF16, kind="Internal")
    _mk(nc, io, "h1_s", [L, 4096], F32, kind="ExternalOutput")
    _mk(nc, io, "hmax_out", [1, 1], F32, kind="ExternalOutput")
    _emit_consts(ctx, io)
    build_l1_ffn1(ctx, io, L, NC)
    kb.final_wait([("h1d", ng, mt) for ng in range(8) for mt in range(L // 128)]
                  + ["hmax_od"])
    errs = kb.check_psum_groups()
    assert not errs, errs[:4]
    kb.finalize()
    return nc


def build_launch3(L, NC):
    nc = bass.Bass("TRN2", target_bir_lowering=False)
    kb = KB(nc)
    ctx = Ctx(nc, kb, bf_words=42000, f32_words=26000, int_words=4096)
    ctx.setup_psum()
    io = {}
    for n, s, d in (("x1", [L, 1024], F32), ("attn1", [L, 1024], F32),
                    ("h1_s", [L, 4096], F32), ("w2", [1024, 4096], F32),
                    ("b2", [1024], F32), ("hmaxs", [1, NC], F32)):
        _mk(nc, io, n, s, d)
    for n in ("ident_bf", "ident_f", "ones_f"):
        s, d = _CONST_SHAPES[n]
        _mk(nc, io, n, s, d)
    _mk(nc, io, "out", [L, 1024], F32, kind="ExternalOutput")
    _emit_consts(ctx, io)
    build_l1_ffn2(ctx, io, L, NC)
    kb.final_wait([("out_done", mt) for mt in range(L // 128)])
    errs = kb.check_psum_groups()
    assert not errs, errs[:4]
    kb.finalize()
    return nc


def _host_layout(inputs):
    f = np.ascontiguousarray
    sh = {
        "wq": f(inputs["L0_attn_Wq"]), "wk": f(inputs["L0_attn_Wk"]),
        "wv": f(inputs["L0_attn_Wv"]), "wo": f(inputs["L0_attn_Wo"]),
        "g1": f(inputs["L0_attn_G1"]), "g2": f(inputs["L0_attn_G2"]),
        "gamma": f(inputs["L0_attn_gamma"]), "beta": f(inputs["L0_attn_beta"]),
        "th_pre": f(np.concatenate([inputs["L0_attn_theta"][:, 0, :].T,
                                    inputs["L0_attn_theta"][:, 0, :].T]).astype(np.float32)),
        "wg": f(inputs["L0_ffn_w_gate"]), "wu": f(inputs["L0_ffn_w_up"]),
        "wd": f(inputs["L0_ffn_w_down"]),
        "steps_raw": f(inputs["L1_attn_steps_raw"]), "G_raw": f(inputs["L1_attn_G_raw"]),
        "A_raw": f(inputs["L1_attn_A_raw"]),
        "B_pre": f(inputs["L1_attn_B_mat"].transpose(1, 0, 2).reshape(1024, 512)),
        "C_pre": f(inputs["L1_attn_C_mat"].transpose(1, 2, 0).reshape(512, 1024)),
        "D_vec": f(inputs["L1_attn_D_vec"]),
        "w1": f(inputs["L1_ffn_w1"]), "b1": f(inputs["L1_ffn_b1"]),
        "w2": f(inputs["L1_ffn_w2"]), "b2": f(inputs["L1_ffn_b2"]),
    }
    sh.update(_consts())
    return sh


def kernel(**inputs):
    L = SEQ
    NC = NCORES
    x = np.ascontiguousarray(inputs["L0_attn_x"], dtype=np.float32)
    sh = _host_layout(inputs)
    l1keys = ["wq", "wk", "wv", "wo", "g1", "g2", "gamma", "beta", "th_pre",
              "wg", "wu", "wd", "steps_raw", "G_raw", "A_raw", "B_pre", "C_pre",
              "D_vec"] + list(_CONST_SHAPES)
    nc1 = build_launch1(L)
    im1 = [dict({k: sh[k] for k in l1keys}, x=x[c]) for c in range(NC)]
    r1 = run_bass_kernel_spmd(nc1, im1, core_ids=list(range(NC)))
    x1 = [np.asarray(r1.results[c]["x1"]) for c in range(NC)]
    attn1 = [np.asarray(r1.results[c]["attn1"]) for c in range(NC)]
    xmaxs = np.concatenate([np.asarray(r1.results[c]["xmax_out"]) for c in range(NC)],
                           axis=1).astype(np.float32)
    nc2 = build_launch2(L, NC)
    l2keys = ["w1", "b1", "ident_bf", "ident_f", "ones_f"]
    im2 = [dict({k: sh[k] for k in l2keys}, x1=x1[c], xmaxs=xmaxs) for c in range(NC)]
    r2 = run_bass_kernel_spmd(nc2, im2, core_ids=list(range(NC)))
    h1 = [np.asarray(r2.results[c]["h1_s"]) for c in range(NC)]
    hmaxs = np.concatenate([np.asarray(r2.results[c]["hmax_out"]) for c in range(NC)],
                           axis=1).astype(np.float32)
    nc3 = build_launch3(L, NC)
    l3keys = ["w2", "b2", "ident_bf", "ident_f", "ones_f"]
    im3 = [dict({k: sh[k] for k in l3keys}, x1=x1[c], attn1=attn1[c], h1_s=h1[c],
                hmaxs=hmaxs) for c in range(NC)]
    r3 = run_bass_kernel_spmd(nc3, im3, core_ids=list(range(NC)))
    out = np.stack([np.asarray(r3.results[c]["out"]) for c in range(NC)], axis=0)
    return out.astype(np.float32)
